# version 12
# baseline (speedup 1.0000x reference)
import contextlib
import numpy as np
import ml_dtypes
import concourse.bass as bass
import concourse.mybir as mybir
from concourse.bass_utils import run_bass_kernel_spmd

F32 = mybir.dt.float32
BF16 = mybir.dt.bfloat16
AF = mybir.ActivationFunctionType
ALU = mybir.AluOpType
AX = mybir.AxisListType
bfdt = ml_dtypes.bfloat16

BATCH, T, D = 32, 2048, 1024
NCORES = 8
SEQN = BATCH // NCORES
NT = T // 128
NCH = D // 128
NEG = -1.0e30

ALL_BLOCKS = ("L0A", "L0F", "L1A", "L1F")
PAIR_ATTN = False


def ts(i, n=128):
    return slice(i * n, (i + 1) * n)


class Eng:
    def __init__(self, e, name):
        self.e = e
        self.name = name
        self.last_wait = {}


class Bld:
    def __init__(self, nc, stack):
        self.nc = nc
        self.stack = stack
        self.t = Eng(nc.tensor, "t")
        self.v = Eng(nc.vector, "v")
        self.s = Eng(nc.scalar, "s")
        self.q = Eng(nc.sync, "q")
        self.g = Eng(nc.gpsimd, "g")
        self.sems = {}
        self.cnt = {}

    def sem(self, name):
        if name not in self.sems:
            self.sems[name] = self.stack.enter_context(self.nc.semaphore(name))
            self.cnt[name] = 0
        return self.sems[name]

    def inc(self, ins, name, k=1):
        ins.then_inc(self.sem(name), k)
        self.cnt[name] += k
        return (name, self.cnt[name])

    def tok(self, name):
        self.sem(name)
        return (name, self.cnt[name])

    def need(self, eng, *tokens):
        best = {}
        for tk in tokens:
            if tk is None:
                continue
            name, val = tk
            if val > best.get(name, 0):
                best[name] = val
        for name, val in best.items():
            if eng.last_wait.get(name, -1) >= val:
                continue
            eng.e.wait_ge(self.sem(name), val)
            eng.last_wait[name] = val

    def dma_barrier(self, *engs):
        toks = [(n, c) for n, c in self.cnt.items() if c > 0]
        for e in engs:
            self.need(e, *toks)

    def dma(self, eng, out, in_, sname, waits=(), **kw):
        self.need(eng, *waits)
        ins = eng.e.dma_start(out=out, in_=in_, **kw)
        return self.inc(ins, sname, 16)


class Bank:
    __slots__ = ("t", "name", "wr", "rd")

    def __init__(self, tens, name):
        self.t = tens
        self.name = name
        self.wr = None
        self.rd = None



def build_program(blocks, n_seq=SEQN):
    nc = bass.Bass("TRN2", target_bir_lowering=False)
    stack = contextlib.ExitStack()

    x_in = nc.dram_tensor("x", [n_seq * T, D], F32, kind="ExternalInput").ap()
    y_out = nc.dram_tensor("y", [n_seq * T, D], F32, kind="ExternalOutput").ap()
    in_names = ["x"]

    def din(name, shape):
        in_names.append(name)
        return nc.dram_tensor(name, shape, F32, kind="ExternalInput").ap()

    W = {}
    shapes = {
        "L0A": [("L0_attn_w_lqkv", [512, 1536]), ("L0_attn_b_lqkv", [1, 1536]),
                ("L0_attn_w_gqkv", [512, 1536]), ("L0_attn_b_gqkv", [1, 1536]),
                ("L0_attn_w_comp", [4096, 1024]), ("L0_attn_b_comp", [1, 1024]),
                ("L0_attn_w_lproj", [512, 512]), ("L0_attn_b_lproj", [1, 512]),
                ("L0_attn_w_gproj", [512, 512]), ("L0_attn_b_gproj", [1, 512]),
                ("L0_attn_w_gate", [1024, 2]), ("L0_attn_b_gate", [1, 2])],
        "L0F": [("L0_ffn_ln_gamma", [1, 1024]), ("L0_ffn_ln_beta", [1, 1024]),
                ("L0_ffn_gate_w", [8, 1024]), ("L0_ffn_w1", [8 * 2048, 1024]),
                ("L0_ffn_w2", [8 * 1024, 2048])],
        "L1A": [("L1_attn_Wqkv", [1024, 3072]), ("L1_attn_bqkv", [1, 3072]),
                ("L1_attn_Wout", [1024, 1024]), ("L1_attn_bout", [1, 1024]),
                ("L1_attn_rel_emb", [257, 16])],
        "L1F": [("L1_ffn_w1_w", [4096, 1024]), ("L1_ffn_w1_s", [1, 4096]),
                ("L1_ffn_w2_w", [4096, 1024]), ("L1_ffn_w2_s", [1, 4096]),
                ("L1_ffn_h1_w", [4096, 4096]), ("L1_ffn_h1_s", [1, 4096]),
                ("L1_ffn_h2_w", [4096, 4096]), ("L1_ffn_h2_s", [1, 4096]),
                ("L1_ffn_w3_w", [1024, 4096]), ("L1_ffn_w3_s", [1, 1024])],
    }
    for blk in blocks:
        for n, sh in shapes[blk]:
            W[n] = din(n, sh)

    c_ident = din("c_ident", [128, 128])
    c_mask = din("c_mask", [128, 128])

    scr = {}
    stg = {}
    if "L0F" in blocks:
        stg["m1"] = nc.dram_tensor("g_m1", [8 * 2048, 1024], BF16).ap()
        stg["m2"] = nc.dram_tensor("g_m2", [8 * 1024, 2048], BF16).ap()
        scr["m1"] = nc.dram_tensor("s_m1", [8 * 2048, 1024], BF16).ap()
        scr["m2"] = nc.dram_tensor("s_m2", [8 * 2048, 1024], BF16).ap()
    if "L1F" in blocks:
        stg["f1"] = nc.dram_tensor("g_f1", [4096, 1024], BF16).ap()
        stg["f2"] = nc.dram_tensor("g_f2", [4096, 1024], BF16).ap()
        stg["h1"] = nc.dram_tensor("g_h1", [4096, 4096], BF16).ap()
        stg["h2"] = nc.dram_tensor("g_h2", [4096, 4096], BF16).ap()
        stg["f3"] = nc.dram_tensor("g_f3", [1024, 4096], BF16).ap()
        scr["f1"] = nc.dram_tensor("s_f1", [4096, 1024], BF16).ap()
        scr["f2"] = nc.dram_tensor("s_f2", [4096, 1024], BF16).ap()
        scr["h1"] = nc.dram_tensor("s_h1", [4096, 4096], BF16).ap()
        scr["h2"] = nc.dram_tensor("s_h2", [4096, 4096], BF16).ap()
        scr["f3"] = nc.dram_tensor("s_f3", [4096, 1024], BF16).ap()
    if "L1A" in blocks:
        scr["erev"] = nc.dram_tensor("s_erev", [16, 400], F32).ap()

    b = Bld(nc, stack)
    t, v, s, q, g = b.t, b.v, b.s, b.q, b.g

    def sbuf(name, shape, dt):
        return stack.enter_context(nc.sbuf_tensor(name, shape, dt))

    resid = sbuf("resid", [128, NT * 1024], BF16)
    xfm = sbuf("xfm", [128, NCH * T], BF16)
    bigA = sbuf("bigA", [128, 16384], BF16)
    bigB = sbuf("bigB", [128, 16384], BF16)
    bigC = sbuf("bigC", [128, 8192], BF16)
    wbuf = sbuf("wbuf", [128, 16384], BF16)
    ssb = sbuf("ssb", [128, 2048], F32)
    pbu = sbuf("pbu", [128, 2048], BF16)
    smal = sbuf("smal", [128, 2048], F32)
    cols = sbuf("cols", [128, 64], F32)
    crow = sbuf("crow", [16, 512], F32)
    cpool = sbuf("cpool", [128, 8], F32)
    sident = sbuf("sident", [128, 128], BF16)
    smask = sbuf("smask", [128, 128], F32)
    sones = sbuf("sones", [1, 512], F32)

    pb = [Bank(stack.enter_context(nc.psum_tensor(f"pb{i}", [128, 512], F32)), f"pb{i}")
          for i in range(8)]
    fident = sbuf("fident", [128, 128], F32)
    gwp = sbuf("gwp", [128, 64], BF16)
    s23 = sbuf("s23", [128, 16], F32)
    e0c = sbuf("e0c", [128, 16], F32)

    CP_EPS, CP_QS, CP_NEG1 = 0, 1, 2

    mtk0 = b.dma(q, smask[:, :], c_mask[:, :], "dqP")
    itk0 = b.dma(q, ssb[:, 0:128], c_ident[:, :], "dqP")
    v.e.memset(cpool[:, CP_EPS:CP_EPS + 1], 1e-5)
    v.e.memset(cpool[:, CP_QS:CP_QS + 1], 0.125)
    v.e.memset(cpool[:, CP_NEG1:CP_NEG1 + 1], -1.0)
    v.e.memset(sones[:, :], 1.0)
    b.need(v, itk0, mtk0)
    v.e.tensor_copy(out=sident[:, :], in_=ssb[:, 0:128])
    consts_tok = b.inc(v.e.tensor_copy(out=fident[:, :], in_=ssb[:, 0:128]), "vm")

    rp_state = {}

    def repack(wsrc, dst, nout, nin, kind, srow=None, src_row0=0, dst_row0=0,
               scol_ix=16, cast_tok=None, scol_sem="dgC3"):
        nci, nco = nin // 128, nout // 128
        assert nin <= 4096
        nats = [wbuf[:, 0:nin], wbuf[:, 4096:4096 + nin]]
        asms = [wbuf[:, 8192:8192 + nin], wbuf[:, 12288:12288 + nin]]
        nat_rd = [rp_state.get("n0"), rp_state.get("n1")]
        asm_rd = [rp_state.get("a0"), rp_state.get("a1")]
        scol_rd = b.tok("vm")
        scol = cols[:, scol_ix:scol_ix + 1]
        for f in range(nco):
            par = f % 2
            nat, asm = nats[par], asms[par]
            ntk = b.dma(g, nat,
                        wsrc[src_row0 + f * 128: src_row0 + (f + 1) * 128, :],
                        "dgN" + str(par), waits=(nat_rd[par],))
            if srow is not None:
                ctk = b.dma(g, scol,
                            srow[0:1, ts(f)].rearrange("a (p c) -> (a p) c", c=1),
                            scol_sem, waits=(scol_rd,), allow_slow_non_contiguous=True)
                b.need(v, ntk, ctk)
                ntk = b.inc(v.e.tensor_scalar(out=nat, in0=nat, scalar1=scol,
                                              scalar2=None, op0=ALU.mult), "vm")
                scol_rd = ntk
            asm_wr = None
            for cg in range(0, nci, 4):
                bank = pb[(cg // 4) % 4]
                b.need(t, ntk, bank.rd, consts_tok)
                nb = min(4, nci - cg)
                for j in range(nb):
                    ins = t.e.matmul(bank.t[:, ts(j)], nat[:, ts(cg + j)],
                                     sident[:, :], start=True, stop=True)
                bank.wr = b.inc(ins, "tm")
                b.need(v, bank.wr, asm_rd[par])
                cp = v.e.tensor_copy(out=asm[:, cg * 128:(cg + nb) * 128],
                                     in_=bank.t[:, 0:nb * 128])
                asm_wr = b.inc(cp, "vm")
                bank.rd = asm_wr
            nat_rd[par] = b.tok("tm")
            if kind == "img":
                asm_rd[par] = b.dma(q, dst[dst_row0 + f * 128: dst_row0 + (f + 1) * 128, :],
                                    asm, "dgA" + str(par), waits=(asm_wr,))
            else:
                for c in range(nci):
                    tk = b.dma(q, dst[dst_row0 + c * 128: dst_row0 + (c + 1) * 128, ts(f)],
                               asm[:, ts(c)], "dgA" + str(par),
                               waits=(asm_wr,) if c == 0 else ())
                asm_rd[par] = tk
        rp_state["n0"], rp_state["n1"] = nat_rd
        rp_state["a0"], rp_state["a1"] = asm_rd

    def prepass():
        if "L1F" in blocks:
            repack(W["L1_ffn_w1_w"], scr["f1"], 4096, 1024, "img")
            repack(W["L1_ffn_w2_w"], scr["f2"], 4096, 1024, "img")
            repack(W["L1_ffn_h1_w"], scr["h1"], 4096, 4096, "img")
            repack(W["L1_ffn_h2_w"], scr["h2"], 4096, 4096, "img")
            repack(W["L1_ffn_w3_w"], scr["f3"], 1024, 4096, "str",
                   srow=W["L1_ffn_w3_s"])
        if "L0F" in blocks:
            for e in range(8):
                repack(W["L0_ffn_w1"], scr["m1"], 2048, 1024, "img",
                       src_row0=e * 2048, dst_row0=e * 2048)
                repack(W["L0_ffn_w2"], scr["m2"], 1024, 2048, "str",
                       src_row0=e * 1024, dst_row0=e * 2048)
            prep_router()
        if "L1A" in blocks:
            prep_relemb()

    def prep_relemb():
        r0 = ssb[:, 1024:1072]
        tk0 = b.dma(q, r0[:, 0:16], W["L1_attn_rel_emb"][0:128, :], "dqE",
                    waits=(consts_tok,))
        tk1 = b.dma(q, r0[:, 16:32], W["L1_attn_rel_emb"][128:256, :], "dqE")
        tk2 = b.dma(q, r0[0:1, 32:48], W["L1_attn_rel_emb"][256:257, :], "dqE")
        b.need(t, tk0, tk2, consts_tok)
        e0tk = b.inc(t.e.matmul(pb[2].t[:, 0:16], sones[0:1, 0:128], r0[0:1, 0:16],
                                start=True, stop=True), "tm")
        pb[2].wr = e0tk
        b.need(v, e0tk)
        e0cp = b.inc(v.e.tensor_copy(out=e0c[:, :], in_=pb[2].t[:, 0:16]), "vm")
        pb[2].rd = e0cp
        b.need(t, tk1, e0cp, pb[4].rd, pb[5].rd)
        tr0 = b.inc(t.e.transpose(pb[4].t[0:16, 0:128], r0[:, 0:16], fident[:, :]), "tm")
        tr1 = b.inc(t.e.transpose(pb[5].t[0:16, 0:128], r0[:, 16:32], fident[:, :]), "tm")
        tr2 = b.inc(t.e.transpose(pb[2].t[0:16, 0:1], r0[0:1, 32:48], fident[0:1, 0:1]), "tm")
        pb[4].wr = pb[5].wr = pb[2].wr = tr2
        b.need(v, tr0, tr1, tr2)
        v.e.tensor_copy(out=crow[0:16, 0:128], in_=pb[4].t[0:16, 0:128])
        v.e.tensor_copy(out=crow[0:16, 128:256], in_=pb[5].t[0:16, 0:128])
        cpk = b.inc(v.e.tensor_copy(out=crow[0:16, 256:257], in_=pb[2].t[0:16, 0:1]), "vm")
        pb[4].rd = pb[5].rd = pb[2].rd = cpk
        b.need(v, cpk)
        rev = crow[0:16, 257:512]
        w1 = b.inc(v.e.tensor_copy(out=smal[0:16, 0:257],
                                   in_=bass.AP(crow, 256, [[512, 16], [-1, 257]])), "vm")
        b.need(v, w1)
        w2 = b.inc(v.e.tensor_scalar(out=smal[0:16, 257:385], in0=smal[0:16, 0:128],
                                     scalar1=0.0, scalar2=crow[0:16, 0:1],
                                     op0=ALU.mult, op1=ALU.add), "vm")
        b.dma(g, scr["erev"][0:16, 0:385], smal[0:16, 0:385], "dgP", waits=(w2,))

    def prep_router():
        gw = ssb[0:8, 0:1024]
        gtk = b.dma(q, gw, W["L0_ffn_gate_w"][:, :], "dqR", waits=(consts_tok,))
        v.e.memset(cpool[:, 3:4], 1.0)
        sc_g = cols[:, 24:25]
        sc_b = cols[:, 25:26]
        un = ssb[:, 1536:1544]
        sg = ssb[:, 1544:1552]
        s2tok = None
        for c in range(NCH):
            pbk = pb[4 + (c % 2)]
            b.need(t, gtk, consts_tok, pbk.rd)
            ttk = b.inc(t.e.transpose(pbk.t[:, 0:8], gw[:, ts(c)], fident[0:8, 0:8]), "tm")
            pbk.wr = ttk
            gtkc = b.dma(g, sc_g, W["L0_ffn_ln_gamma"][0:1, ts(c)].rearrange(
                "a (p d) -> (a p) d", d=1), "dgC", allow_slow_non_contiguous=True)
            btkc = b.dma(g, sc_b, W["L0_ffn_ln_beta"][0:1, ts(c)].rearrange(
                "a (p d) -> (a p) d", d=1), "dgC", allow_slow_non_contiguous=True)
            b.need(v, ttk, gtkc, btkc, s2tok)
            cp1 = b.inc(v.e.tensor_copy(out=un, in_=pbk.t[:, 0:8]), "vm")
            b.need(v, cp1)
            cp2 = b.inc(v.e.tensor_scalar(out=sg, in0=un, scalar1=sc_g, scalar2=None,
                                          op0=ALU.mult), "vm")
            pbk.rd = cp2
            b.need(v, cp2)
            cp3 = b.inc(v.e.tensor_copy(out=gwp[:, c * 8:(c + 1) * 8], in_=sg), "vm")
            b.need(t, cp2, cp3)
            t.e.matmul(pb[0].t[0:1, 0:8], cpool[:, 3:4], sg,
                       start=(c == 0), stop=(c == NCH - 1))
            ins = t.e.matmul(pb[1].t[0:1, 0:8], sc_b, un,
                             start=(c == 0), stop=(c == NCH - 1))
            s2tok = b.inc(ins, "tm")
            b.need(g, s2tok)
        b.need(v, s2tok)
        v.e.tensor_copy(out=crow[0:1, 0:8], in_=pb[0].t[0:1, 0:8])
        rtk = b.inc(v.e.tensor_copy(out=crow[0:1, 8:16], in_=pb[1].t[0:1, 0:8]), "vm")
        pb[0].rd = rtk
        pb[1].rd = rtk
        b.need(t, rtk)
        t.e.matmul(pb[0].t[:, 0:8], sones[0:1, 0:128], crow[0:1, 0:8],
                   start=True, stop=True)
        btk = b.inc(t.e.matmul(pb[1].t[:, 0:8], sones[0:1, 0:128], crow[0:1, 8:16],
                               start=True, stop=True), "tm")
        pb[0].wr = btk
        pb[1].wr = btk
        b.need(v, btk)
        v.e.tensor_copy(out=s23[:, 0:8], in_=pb[0].t[:, 0:8])
        ctk = b.inc(v.e.tensor_copy(out=s23[:, 8:16], in_=pb[1].t[:, 0:8]), "vm")
        pb[0].rd = ctk
        pb[1].rd = ctk

    def load_x(seq):
        dst = bass.AP(resid, 0, [[NT * 1024, 128], [1024, NT], [1, 1024]])
        sap = bass.AP(x_in.tensor, seq * T * D, [[1024, 128], [128 * 1024, NT], [1, 1024]])
        waits = (b.tok("vm"), b.tok("sm"), b.tok("tm")) if seq > 0 else ()
        return b.dma(g, dst, sap, "dgx", waits=waits)

    def make_xfm(xtok, chunks=NCH):
        stk = None
        get = xtok if callable(xtok) else (lambda ti: xtok)
        for ti in range(NT):
            for half in range(chunks // 4):
                bank = pb[(2 * ti + half) % 4]
                b.need(t, get(ti), bank.rd, consts_tok)
                for j in range(4):
                    c = half * 4 + j
                    ins = t.e.matmul(bank.t[:, ts(j)],
                                     resid[:, ti * 1024 + c * 128: ti * 1024 + (c + 1) * 128],
                                     sident[:, :], start=True, stop=True)
                bank.wr = b.inc(ins, "tm")
                b.need(s, bank.wr)
                dst = bass.AP(xfm, half * 4 * T + ti * 128,
                              [[NCH * T, 128], [T, 4], [1, 128]])
                stk = b.inc(s.e.copy(out=dst, in_=bank.t[:, 0:512]), "sm")
                bank.rd = stk
        return stk

    def xsl(c, tg):
        return xfm[:, c * T + tg * 512:c * T + (tg + 1) * 512]

    def xslt(c, ti):
        return xfm[:, c * T + ti * 128:c * T + (ti + 1) * 128]

    ep_state = {"apply": None, "dy": None}

    def ln_epilogue(seq, ti, src0, src1, final, banks=()):
        sc = cols
        ssl = ssb[:, 0:1024]
        b.need(v, src0[1], src1[1], ep_state["apply"])
        a0 = b.inc(v.e.scalar_tensor_tensor(
            out=ssl[:, 0:512], in0=src0[0], scalar=1.0,
            in1=resid[:, ti * 1024: ti * 1024 + 512], op0=ALU.mult, op1=ALU.add,
            accum_out=sc[:, 0:1]), "vm")
        a1 = b.inc(v.e.scalar_tensor_tensor(
            out=ssl[:, 512:1024], in0=src1[0], scalar=1.0,
            in1=resid[:, ti * 1024 + 512: (ti + 1) * 1024], op0=ALU.mult, op1=ALU.add,
            accum_out=sc[:, 1:2]), "vm")
        for bk in banks:
            bk.rd = a1
        b.need(v, a0, a1)
        q0 = b.inc(v.e.scalar_tensor_tensor(
            out=pbu[:, 0:512], in0=ssl[:, 0:512], scalar=1.0, in1=ssl[:, 0:512],
            op0=ALU.mult, op1=ALU.mult, accum_out=sc[:, 2:3]), "vm")
        q1 = b.inc(v.e.scalar_tensor_tensor(
            out=pbu[:, 512:1024], in0=ssl[:, 512:1024], scalar=1.0, in1=ssl[:, 512:1024],
            op0=ALU.mult, op1=ALU.mult, accum_out=sc[:, 3:4]), "vm")
        b.need(v, a0, a1, q0, q1)
        v.e.tensor_add(out=sc[:, 4:5], in0=sc[:, 0:1], in1=sc[:, 1:2])
        m1 = b.inc(v.e.tensor_add(out=sc[:, 5:6], in0=sc[:, 2:3], in1=sc[:, 3:4]), "vm")
        b.need(v, m1)
        v.e.tensor_scalar(out=sc[:, 6:7], in0=sc[:, 4:5], scalar1=1.0 / 1024,
                          scalar2=None, op0=ALU.mult)
        m2 = b.inc(v.e.tensor_scalar(out=sc[:, 7:8], in0=sc[:, 5:6], scalar1=1.0 / 1024,
                                     scalar2=None, op0=ALU.mult), "vm")
        b.need(v, m2)
        m3 = b.inc(v.e.tensor_mul(out=sc[:, 8:9], in0=sc[:, 6:7], in1=sc[:, 6:7]), "vm")
        b.need(v, m3)
        m4 = b.inc(v.e.tensor_sub(out=sc[:, 9:10], in0=sc[:, 7:8], in1=sc[:, 8:9]), "vm")
        b.need(s, m4)
        m5 = b.inc(s.e.activation(out=sc[:, 10:11], in_=sc[:, 9:10], func=AF.Sqrt,
                                  bias=cpool[:, CP_EPS:CP_EPS + 1], scale=1.0), "sm")
        b.need(v, m5)
        m6 = b.inc(v.e.reciprocal(out=sc[:, 11:12], in_=sc[:, 10:11]), "vm")
        b.need(v, m6)
        ap_tok = b.inc(v.e.tensor_scalar(out=resid[:, ti * 1024:(ti + 1) * 1024],
                                         in0=ssl, scalar1=sc[:, 6:7], scalar2=sc[:, 11:12],
                                         op0=ALU.subtract, op1=ALU.mult), "vm")
        if final:
            b.need(v, ep_state["dy"])
            of_tok = b.inc(v.e.tensor_scalar(out=smal[:, 0:1024],
                                             in0=ssl, scalar1=sc[:, 6:7], scalar2=sc[:, 11:12],
                                             op0=ALU.subtract, op1=ALU.mult), "vm")
            dst = bass.AP(y_out.tensor, seq * T * D + ti * 128 * 1024,
                          [[1024, 128], [1, 1024]])
            ep_state["dy"] = b.dma(q, dst, smal[:, 0:1024], "dy", waits=(of_tok,))
            ap_tok = of_tok
        ep_state["apply"] = ap_tok
        ep_state.setdefault("tiles", {})[ti] = ap_tok
        return ap_tok

    def block_l0a(seq, final):
        b.dma_barrier(q, g)
        tls = ep_state.get("tiles", {})
        fallback = b.tok("vm")
        xftok = make_xfm(lambda ti: tls.get(ti, fallback))
        st = {"qkv_tok": None}

        def diag_m(h):
            return smask[:, :]

        def out_fm_l(h, qg):
            hp = h // 2
            return bigA[:, hp * 2048 + qg * 512:hp * 2048 + (qg + 1) * 512]

        def load_lqkv(g2):
            base = (g2 % 2) * 1536
            toks = []
            for j, col0 in enumerate((g2 * 128, 512 + g2 * 128, 1024 + g2 * 128)):
                dst = bass.AP(wbuf, base + j * 512, [[16384, 128], [128, 4], [1, 128]])
                srcw = bass.AP(W["L0_attn_w_lqkv"].tensor, col0,
                               [[1536, 128], [1536 * 128, 4], [1, 128]])
                toks.append(b.dma(g, dst, srcw, "gl", waits=(b.tok("tm"),) if j == 0 else ()))
            rbs = 1280 + (g2 % 2) * 384
            btk = None
            for j, col0 in enumerate((g2 * 128, 512 + g2 * 128, 1024 + g2 * 128)):
                btk = b.dma(q, smal[0:1, rbs + j * 128:rbs + (j + 1) * 128],
                            W["L0_attn_b_lqkv"][0:1, col0:col0 + 128], "qr",
                            waits=(b.tok("tm"), b.tok("qr")) if j == 0 else ())
            return toks[-1], btk, rbs

        def lqkv_fillers(g2, wtok, btk, rbs):
            base = (g2 % 2) * 1536
            bufb = (g2 % 2) * 6144
            fill = []
            for j, scale in ((0, True), (1, False)):
                for tg in range(4):
                    def go(j=j, tg=tg, scale=scale):
                        bank = pb[6]
                        b.need(t, wtok, btk, xftok, bank.rd)
                        for c in range(4):
                            t.e.matmul(bank.t[:, :],
                                       wbuf[:, base + j * 512 + c * 128:
                                            base + j * 512 + (c + 1) * 128],
                                       xsl(c, tg), start=(c == 0), stop=False)
                        ins = t.e.matmul(bank.t[:, :],
                                         smal[0:1, rbs + j * 128:rbs + (j + 1) * 128],
                                         sones[0:1, 0:512], start=False, stop=True)
                        mtk = b.inc(ins, "tm")
                        bank.wr = mtk
                        b.need(v, mtk)
                        dstc = bigB[:, bufb + j * 2048 + tg * 512:
                                    bufb + j * 2048 + (tg + 1) * 512]
                        if scale:
                            etk = b.inc(v.e.tensor_scalar(out=dstc, in0=bank.t[:, :],
                                                          scalar1=0.125, scalar2=None,
                                                          op0=ALU.mult), "vm")
                        else:
                            etk = b.inc(v.e.tensor_copy(out=dstc, in_=bank.t[:, :]), "vm")
                        bank.rd = etk
                    fill.append(go)
            for ti in range(NT):
                def gov(ti=ti):
                    bank = pb[6]
                    b.need(t, wtok, btk, xftok, bank.rd)
                    for c in range(4):
                        t.e.matmul(bank.t[:, 0:128], xslt(c, ti),
                                   wbuf[:, base + 1024 + c * 128:base + 1024 + (c + 1) * 128],
                                   start=(c == 0), stop=False)
                    ins = t.e.matmul(bank.t[:, 0:128], sones[0:1, 0:128],
                                     smal[0:1, rbs + 256:rbs + 384], start=False, stop=True)
                    mtk = b.inc(ins, "tm")
                    bank.wr = mtk
                    b.need(v, mtk)
                    etk = b.inc(v.e.tensor_copy(
                        out=bigB[:, bufb + 4096 + ti * 128:bufb + 4096 + (ti + 1) * 128],
                        in_=bank.t[:, 0:128]), "vm")
                    bank.rd = etk
                fill.append(gov)
            return fill

        w0l, b0l, rb0l = load_lqkv(0)
        for go in lqkv_fillers(0, w0l, b0l, rb0l):
            go()
        for g2 in range(4):
            st["qkv_tok"] = b.tok("vm")
            if g2 < 3:
                wn, bn, rbn = load_lqkv(g2 + 1)
                fillers = lqkv_fillers(g2 + 1, wn, bn, rbn)
            else:
                fillers = []
            bufb = (g2 % 2) * 6144
            heads = [(g2 * 2 + j, 64 * j, 0, j) for j in range(2)]
            attn_softmax_pv(seq, heads, T, bigB, bufb, bufb + 2048, bigB,
                            bufb + 4096, 128,
                            [(bigC, 0, 8192), (bigC, 4096, 8192)],
                            out_fm_l, diag_m, None, None, st,
                            (wbuf, 6144), wbuf[:, 8192:12288].bitcast(F32),
                            wbuf[:, 14336:16384].bitcast(F32),
                            wbuf[:, 12288:14336].bitcast(F32), filler=fillers)
        lat_tok = b.tok("vm")

        def lproj_fillers():
            wl_rd = b.tok("tm")
            dstw = bass.AP(wbuf, 0, [[16384, 128], [512, 4], [1, 512]])
            srcw = bass.AP(W["L0_attn_w_lproj"].tensor, 0,
                           [[512, 128], [512 * 128, 4], [1, 512]])
            wltk = b.dma(g, dstw, srcw, "gp", waits=(wl_rd, b.tok("vm"), b.tok("sm")))
            bltk = b.dma(q, smal[0:1, 1280:1792], W["L0_attn_b_lproj"][0:1, :], "qr",
                         waits=(b.tok("tm"), b.tok("qr")))
            fill = []
            for oc in range(4):
                for tg in range(4):
                    def go(oc=oc, tg=tg):
                        bank = pb[6]
                        b.need(t, wltk, bltk, lat_tok, bank.rd)
                        for c in range(4):
                            t.e.matmul(bank.t[:, :],
                                       wbuf[:, c * 512 + oc * 128:c * 512 + (oc + 1) * 128],
                                       bigA[:, c * 2048 + tg * 512:c * 2048 + (tg + 1) * 512],
                                       start=(c == 0), stop=False)
                        ins = t.e.matmul(bank.t[:, :],
                                         smal[0:1, 1280 + oc * 128:1280 + (oc + 1) * 128],
                                         sones[0:1, 0:512], start=False, stop=True)
                        mtk = b.inc(ins, "tm")
                        bank.wr = mtk
                        b.need(v, mtk)
                        etk = b.inc(v.e.tensor_copy(
                            out=bigB[:, oc * 2048 + tg * 512:oc * 2048 + (tg + 1) * 512],
                            in_=bank.t[:, :]), "vm")
                        bank.rd = etk
                    fill.append(go)
            return fill

        wc_rd = b.tok("tm")
        ctoks = []
        for r in range(4):
            dst = bass.AP(wbuf, r * 4096, [[16384, 128], [512, 8], [1, 512]])
            src = bass.AP(W["L0_attn_w_comp"].tensor, r * 1024 * 1024 + 512,
                          [[1024, 128], [1024 * 128, 8], [1, 512]])
            ctoks.append(b.dma(g, dst, src, "gc", waits=(wc_rd, b.tok("vm"), b.tok("sm")) if r == 0 else ()))
        bctk = b.dma(q, smal[0:1, 1280:1792], W["L0_attn_b_comp"][0:1, 512:1024],
                     "qr", waits=(b.tok("tm"), b.tok("qr")))
        for oc in range(4):
            bank = pb[4]
            b.need(t, ctoks[-1], bctk, xftok, bank.rd)
            for r in range(4):
                for c in range(NCH):
                    t.e.matmul(bank.t[:, :],
                               wbuf[:, r * 4096 + c * 512 + oc * 128:
                                    r * 4096 + c * 512 + (oc + 1) * 128],
                               bass.AP(xfm, c * T + r, [[NCH * T, 128], [4, 512]]),
                               start=(r == 0 and c == 0), stop=False)
            ins = t.e.matmul(bank.t[:, :],
                             smal[0:1, 1280 + oc * 128:1280 + (oc + 1) * 128],
                             sones[0:1, 0:512], start=False, stop=True)
            mtk = b.inc(ins, "tm")
            bank.wr = mtk
            b.need(v, mtk)
            etk = b.inc(v.e.tensor_copy(out=bigC[:, oc * 512:(oc + 1) * 512],
                                        in_=bank.t[:, :]), "vm")
            bank.rd = etk
        cg_tok = b.tok("vm")

        wg_rd = b.tok("tm")
        dst = bass.AP(wbuf, 0, [[16384, 128], [1536, 4], [1, 1536]])
        src = bass.AP(W["L0_attn_w_gqkv"].tensor, 0, [[1536, 128], [1536 * 128, 4], [1, 1536]])
        wgtk = b.dma(g, dst, src, "gg", waits=(wg_rd, b.tok("vm"), b.tok("sm")))
        bgtk = b.dma(q, smal[0:1, 0:1536], W["L0_attn_b_gqkv"][0:1, :], "qr",
                     waits=(b.tok("tm"), b.tok("vm"), b.tok("qr")))
        for j, (dst_col, scale) in enumerate(((2048, True), (4096, False))):
            for d2 in range(4):
                bank = pb[(d2 % 2)]
                b.need(t, wgtk, bgtk, cg_tok, bank.rd)
                for c in range(4):
                    t.e.matmul(bank.t[:, :],
                               wbuf[:, c * 1536 + j * 512 + d2 * 128:
                                    c * 1536 + j * 512 + (d2 + 1) * 128],
                               bigC[:, c * 512:(c + 1) * 512],
                               start=(c == 0), stop=False)
                ins = t.e.matmul(bank.t[:, :],
                                 smal[0:1, j * 512 + d2 * 128:j * 512 + (d2 + 1) * 128],
                                 sones[0:1, 0:512], start=False, stop=True)
                mtk = b.inc(ins, "tm")
                bank.wr = mtk
                b.need(v, mtk)
                dstc = bigC[:, dst_col + d2 * 512:dst_col + (d2 + 1) * 512]
                if scale:
                    etk = b.inc(v.e.tensor_scalar(out=dstc, in0=bank.t[:, :],
                                                  scalar1=0.125, scalar2=None,
                                                  op0=ALU.mult), "vm")
                else:
                    etk = b.inc(v.e.tensor_copy(out=dstc, in_=bank.t[:, :]), "vm")
                bank.rd = etk
        for tct in range(4):
            bank = pb[2 + (tct % 2)]
            b.need(t, wgtk, bgtk, cg_tok, bank.rd)
            for c in range(4):
                t.e.matmul(bank.t[:, :],
                           bigC[:, c * 512 + tct * 128:c * 512 + (tct + 1) * 128],
                           wbuf[:, c * 1536 + 1024:c * 1536 + 1536],
                           start=(c == 0), stop=False)
            ins = t.e.matmul(bank.t[:, :], sones[0:1, 0:128],
                             smal[0:1, 1024:1536], start=False, stop=True)
            mtk = b.inc(ins, "tm")
            bank.wr = mtk
            b.need(v, mtk)
            etk = b.inc(v.e.tensor_copy(out=bigC[:, 6144 + tct * 512:6144 + (tct + 1) * 512],
                                        in_=bank.t[:, :]), "vm")
            bank.rd = etk
        st["qkv_tok"] = b.tok("vm")

        def out_fm_g(h, qg):
            hp = h // 2
            return bigB[:, 12288 + hp * 512:12288 + (hp + 1) * 512]

        heads_g = [(h, 64 * (h % 2), h // 2, h) for h in range(8)]
        lp_fill = lproj_fillers()
        attn_softmax_pv(seq, heads_g, 512, bigC, 2048, 4096, bigC, 6144, 512,
                        [(wbuf, 8192, 16384), (bigB, 8192, 16384)],
                        out_fm_g, diag_m, None, None, st,
                        (wbuf, 10240), wbuf[:, 12288:13312].bitcast(F32),
                        wbuf[:, 15360:16384].bitcast(F32),
                        wbuf[:, 13312:15360].bitcast(F32), filler=lp_fill)
        gat_tok = b.tok("vm")

        wp_rd = b.tok("tm")
        dst = bass.AP(wbuf, 0, [[16384, 128], [512, 4], [1, 512]])
        src = bass.AP(W["L0_attn_w_gproj"].tensor, 0, [[512, 128], [512 * 128, 4], [1, 512]])
        wptk = b.dma(g, dst, src, "gp", waits=(wp_rd, b.tok("vm"), b.tok("sm")))
        bptk = b.dma(q, smal[0:1, 1280:1792], W["L0_attn_b_gproj"][0:1, :], "qr",
                     waits=(b.tok("tm"), b.tok("qr")))
        for oc in range(4):
            bank = pb[(oc % 2)]
            b.need(t, wptk, bptk, gat_tok, bank.rd)
            for c in range(4):
                t.e.matmul(bank.t[:, :],
                           wbuf[:, c * 512 + oc * 128:c * 512 + (oc + 1) * 128],
                           bigB[:, 12288 + c * 512:12288 + (c + 1) * 512],
                           start=(c == 0), stop=False)
            ins = t.e.matmul(bank.t[:, :],
                             smal[0:1, 1280 + oc * 128:1280 + (oc + 1) * 128],
                             sones[0:1, 0:512], start=False, stop=True)
            mtk = b.inc(ins, "tm")
            bank.wr = mtk
            b.need(v, mtk)
            etk = b.inc(v.e.tensor_copy(out=bigB[:, 14336 + oc * 512:14336 + (oc + 1) * 512],
                                        in_=bank.t[:, :]), "vm")
            bank.rd = etk
        b.need(v, b.tok("vm"))
        rep_tok = None
        for ch in range(4):
            for r in range(4):
                dstr = bass.AP(bigA, 8192 + ch * 2048 + r, [[16384, 128], [4, 512]])
                rep_tok = b.inc(v.e.tensor_copy(
                    out=dstr, in_=bigB[:, 14336 + ch * 512:14336 + (ch + 1) * 512]), "vm")

        wgt_rd = b.tok("tm")
        dstg = bass.AP(wbuf, 0, [[16384, 128], [2, 8], [1, 2]])
        srcg = bass.AP(W["L0_attn_w_gate"].tensor, 0, [[2, 128], [2 * 128, 8], [1, 2]])
        wgttk = b.dma(g, dstg, srcg, "gt", waits=(wgt_rd, b.tok("vm"), b.tok("sm")))
        bgttk = b.dma(q, smal[0:1, 1900:1902], W["L0_attn_b_gate"][0:1, :], "qr",
                      waits=(b.tok("tm"), b.tok("qr")))
        for tgg in range(4):
            for e in range(2):
                bank = pb[4 + e]
                b.need(t, wgttk, bgttk, rep_tok, bank.rd)
                for c in range(8):
                    srcc = (bigB[:, c * 2048 + tgg * 512:c * 2048 + (tgg + 1) * 512]
                            if c < 4 else
                            bigA[:, 8192 + (c - 4) * 2048 + tgg * 512:
                                 8192 + (c - 4) * 2048 + (tgg + 1) * 512])
                    t.e.matmul(bank.t[0:1, :], wbuf[:, c * 2 + e:c * 2 + e + 1],
                               srcc, start=(c == 0), stop=False)
                ins = t.e.matmul(bank.t[0:1, :], smal[0:1, 1900 + e:1901 + e],
                                 sones[0:1, 0:512], start=False, stop=True)
                bank.wr = b.inc(ins, "tm")
            b.need(v, pb[4].wr, pb[5].wr)
            c1 = b.inc(v.e.tensor_copy(out=ssb[0:1, 1024:1536], in_=pb[5].t[0:1, :]), "vm")
            pb[5].rd = c1
            b.need(v, c1)
            c2 = b.inc(v.e.scalar_tensor_tensor(out=ssb[0:1, 1536:2048],
                                                in0=pb[4].t[0:1, :], scalar=1.0,
                                                in1=ssb[0:1, 1024:1536],
                                                op0=ALU.mult, op1=ALU.subtract), "vm")
            pb[4].rd = c2
            b.need(s, c2)
            c3 = b.inc(s.e.activation(out=smal[0:1, 1024:1536],
                                      in_=ssb[0:1, 1536:2048], func=AF.Sigmoid), "sm")
            b.need(v, c3)
            c4 = b.inc(v.e.tensor_scalar(out=smal[0:1, 1536:2048],
                                         in0=smal[0:1, 1024:1536],
                                         scalar1=-1.0, scalar2=1.0,
                                         op0=ALU.mult, op1=ALU.add), "vm")
            for half, (buf, base, gcol) in enumerate(((bigB, 0, 1024), (bigA, 8192, 1536))):
                for ch in range(4):
                    bank = pb[(ch % 2)]
                    b.need(t, c3 if half == 0 else c4, bank.rd)
                    mtk = b.inc(t.e.matmul(bank.t[:, :], sones[0:1, 0:128],
                                           smal[0:1, gcol:gcol + 512],
                                           start=True, stop=True), "tm")
                    bank.wr = mtk
                    b.need(v, mtk, b.tok("vm"))
                    sl = buf[:, base + ch * 2048 + tgg * 512:base + ch * 2048 + (tgg + 1) * 512]
                    etk = b.inc(v.e.tensor_mul(out=sl, in0=sl, in1=bank.t[:, :]), "vm")
                    bank.rd = etk
        bo_tok = b.tok("vm")

        for ti in range(NT):
            for half, bank in ((0, pb[6]), (1, pb[7])):
                b.need(t, bo_tok, bank.rd, consts_tok)
                for j in range(4):
                    c = half * 4 + j
                    srcc = (bigB[:, c * 2048 + ti * 128:c * 2048 + (ti + 1) * 128]
                            if c < 4 else
                            bigA[:, 8192 + (c - 4) * 2048 + ti * 128:
                                 8192 + (c - 4) * 2048 + (ti + 1) * 128])
                    ins = t.e.matmul(bank.t[:, ts(j)], srcc, sident[:, :],
                                     start=True, stop=True)
                bank.wr = b.inc(ins, "tm")
            ln_epilogue(seq, ti, (pb[6].t[:, :], pb[6].wr),
                        (pb[7].t[:, :], pb[7].wr), final, banks=(pb[6], pb[7]))

    RT = 1160

    def router_tile(seq, ti, xftok):
        sc = cols
        rs = resid[:, ti * 1024:(ti + 1) * 1024]
        b.need(v, ep_state["apply"])
        t0 = b.inc(v.e.tensor_reduce(out=sc[:, 0:1], in_=rs, axis=AX.X, op=ALU.add), "vm")
        t1 = b.inc(v.e.scalar_tensor_tensor(out=pbu[:, 0:1024], in0=rs, scalar=1.0,
                                            in1=rs, op0=ALU.mult, op1=ALU.mult,
                                            accum_out=sc[:, 1:2]), "vm")
        b.need(v, t0, t1)
        v.e.tensor_scalar(out=sc[:, 2:3], in0=sc[:, 0:1], scalar1=1.0 / 1024,
                          scalar2=None, op0=ALU.mult)
        t2 = b.inc(v.e.tensor_scalar(out=sc[:, 3:4], in0=sc[:, 1:2], scalar1=1.0 / 1024,
                                     scalar2=None, op0=ALU.mult), "vm")
        b.need(v, t2)
        t3 = b.inc(v.e.tensor_mul(out=sc[:, 4:5], in0=sc[:, 2:3], in1=sc[:, 2:3]), "vm")
        b.need(v, t3)
        t4 = b.inc(v.e.tensor_sub(out=sc[:, 5:6], in0=sc[:, 3:4], in1=sc[:, 4:5]), "vm")
        b.need(s, t4)
        t5 = b.inc(s.e.activation(out=sc[:, 6:7], in_=sc[:, 5:6], func=AF.Sqrt,
                                  bias=cpool[:, 0:1], scale=1.0), "sm")
        b.need(v, t5)
        t6 = b.inc(v.e.reciprocal(out=sc[:, 7:8], in_=sc[:, 6:7]), "vm")
        b.need(v, t6)
        t7 = b.inc(v.e.scalar_tensor_tensor(out=sc[:, 8:9], in0=sc[:, 2:3], scalar=-1.0,
                                            in1=sc[:, 7:8], op0=ALU.mult, op1=ALU.mult), "vm")
        bank = pb[4 + (ti % 2)]
        b.need(t, xftok, bank.rd)
        for c in range(NCH):
            ins = t.e.matmul(bank.t[:, 0:8], xslt(c, ti), gwp[:, c * 8:(c + 1) * 8],
                             start=(c == 0), stop=(c == NCH - 1))
        mtk = b.inc(ins, "tm")
        bank.wr = mtk
        lgA = smal[:, RT:RT + 8]
        lgB = smal[:, RT + 8:RT + 16]
        lg = smal[:, RT + 16:RT + 24]
        pr = smal[:, RT + 24:RT + 32]
        gg = smal[:, RT + 32:RT + 40]
        mk = smal[:, RT + 40:RT + 48]
        b.need(v, mtk, t6, t7)
        u0 = b.inc(v.e.tensor_scalar(out=lgA, in0=bank.t[:, 0:8], scalar1=sc[:, 7:8],
                                     scalar2=None, op0=ALU.mult), "vm")
        bank.rd = u0
        b.need(v, u0)
        u1 = b.inc(v.e.scalar_tensor_tensor(out=lgB, in0=s23[:, 0:8], scalar=sc[:, 8:9],
                                            in1=lgA, op0=ALU.mult, op1=ALU.add), "vm")
        b.need(v, u1)
        u2 = b.inc(v.e.tensor_add(out=lg, in0=lgB, in1=s23[:, 8:16]), "vm")
        b.need(v, u2)
        u3 = b.inc(v.e.tensor_reduce(out=sc[:, 9:10], in_=lg, axis=AX.X, op=ALU.max,
                                     negate=True), "vm")
        b.need(s, u2, u3)
        u4 = b.inc(s.e.activation(out=pr, in_=lg, func=AF.Exp, bias=sc[:, 9:10],
                                  scale=1.0, accum_out=sc[:, 10:11]), "sm")
        b.need(v, u4)
        u5 = b.inc(v.e.reciprocal(out=sc[:, 11:12], in_=sc[:, 10:11]), "vm")
        b.need(v, u5)
        u6 = b.inc(v.e.tensor_scalar(out=pr, in0=pr, scalar1=sc[:, 11:12],
                                     scalar2=None, op0=ALU.mult), "vm")
        b.need(v, u6)
        u7 = b.inc(v.e.tensor_reduce(out=sc[:, 12:13], in_=pr, axis=AX.X, op=ALU.max), "vm")
        b.need(v, u7)
        u8 = b.inc(v.e.tensor_scalar(out=gg, in0=pr, scalar1=sc[:, 12:13],
                                     scalar2=None, op0=ALU.is_ge), "vm")
        b.need(v, u8)
        u9 = b.inc(v.e.scalar_tensor_tensor(out=mk, in0=gg, scalar=NEG, in1=pr,
                                            op0=ALU.mult, op1=ALU.add), "vm")
        b.need(v, u9)
        uA = b.inc(v.e.tensor_reduce(out=sc[:, 13:14], in_=mk, axis=AX.X, op=ALU.max), "vm")
        b.need(v, uA)
        uB = b.inc(v.e.tensor_scalar(out=gg, in0=pr, scalar1=sc[:, 13:14],
                                     scalar2=None, op0=ALU.is_ge), "vm")
        b.need(v, uB)
        uC = b.inc(v.e.tensor_mul(out=smal[:, 1024 + ti * 8:1024 + (ti + 1) * 8],
                                  in0=gg, in1=pr), "vm")
        return uC

    def block_l0f(seq, final):
        b.dma_barrier(q, g)
        tls = ep_state.get("tiles", {})
        fallback = b.tok("vm")
        xftok = make_xfm(lambda ti: tls.get(ti, fallback))
        ro_done = [None] * 4
        def ld_w1(e, waits):
            tk = None
            for f4 in range(4):
                tk = b.dma(q, bass.AP(wbuf, f4 * 4096, [[16384, 128], [1024, 4], [1, 1024]]),
                           bass.AP(scr["m1"].tensor, (e * 2048 + f4 * 512) * 1024,
                                   [[1024, 128], [128 * 1024, 4], [1, 1024]]),
                           "qa", waits=waits if f4 == 0 else ())
            return tk

        def ld_w2(e, waits):
            tk = None
            for f4 in range(4):
                tk = b.dma(q, bass.AP(bigB, f4 * 4096, [[16384, 128], [1024, 4], [1, 1024]]),
                           bass.AP(scr["m2"].tensor, (e * 2048 + f4 * 512) * 1024,
                                   [[1024, 128], [128 * 1024, 4], [1, 1024]]),
                           "qb", waits=waits if f4 == 0 else ())
            return tk

        w1tks = [None] * 8
        w2tks = [None] * 8
        w1tks[0] = ld_w1(0, (b.tok("tm"),))
        w2tks[0] = ld_w2(0, (b.tok("tm"),))
        for e in range(8):
            w1tk = w1tks[e]
            w2tk = w2tks[e]
            for tg in range(4):
                if e == 0:
                    for ti in range(tg * 4, tg * 4 + 4):
                        ro_done[tg] = router_tile(seq, ti, xftok)
                for f in range(16):
                    bank = pb[f % 2]
                    b.need(t, w1tk, xftok, bank.rd)
                    for c in range(NCH):
                        ins = t.e.matmul(bank.t[:, :], wbuf[:, f * 1024 + c * 128:f * 1024 + (c + 1) * 128],
                                         xsl(c, tg), start=(c == 0), stop=(c == NCH - 1))
                    mtk = b.inc(ins, "tm")
                    bank.wr = mtk
                    b.need(s, mtk)
                    etk = b.inc(s.e.activation(out=bigC[:, f * 512:(f + 1) * 512],
                                               in_=bank.t[:, :], func=AF.Gelu), "sm")
                    bank.rd = etk
                hdone = b.tok("sm")
                if tg == 3 and e < 7:
                    w1tks[e + 1] = ld_w1(e + 1, (b.tok("tm"),))
                for tt in range(4):
                    ti = tg * 4 + tt
                    comb_col = smal[:, 1024 + ti * 8 + e:1024 + ti * 8 + e + 1]
                    for ch in range(2):
                        bank = pb[2 + ch]
                        b.need(t, w2tk, bank.rd, hdone)
                        for f in range(16):
                            ins = t.e.matmul(bank.t[:, :],
                                             bigC[:, f * 512 + tt * 128:f * 512 + (tt + 1) * 128],
                                             bigB[:, f * 1024 + ch * 512:f * 1024 + (ch + 1) * 512],
                                             start=(f == 0), stop=(f == 15))
                        mtk = b.inc(ins, "tm")
                        bank.wr = mtk
                        dstc = bigA[:, ti * 1024 + ch * 512:ti * 1024 + (ch + 1) * 512]
                        b.need(v, mtk)
                        if e == 0:
                            b.need(v, ro_done[tg])
                            etk = b.inc(v.e.tensor_scalar(out=dstc, in0=bank.t[:, :],
                                                          scalar1=comb_col, scalar2=None,
                                                          op0=ALU.mult), "vm")
                        else:
                            b.need(v, b.tok("vm"))
                            etk = b.inc(v.e.scalar_tensor_tensor(out=dstc, in0=bank.t[:, :],
                                                                 scalar=comb_col, in1=dstc,
                                                                 op0=ALU.mult, op1=ALU.add), "vm")
                        bank.rd = etk
            if e < 7:
                w2tks[e + 1] = ld_w2(e + 1, (b.tok("tm"),))
        acc_tok = b.tok("vm")
        for ti in range(NT):
            ln_epilogue(seq, ti, (bigA[:, ti * 1024:ti * 1024 + 512], acc_tok),
                        (bigA[:, ti * 1024 + 512:(ti + 1) * 1024], acc_tok), final)

    def attn_softmax_pv(seq, heads, Tq, qk_buf, qbuf_col, kbuf_col, v_buf,
                        vbuf_col, vstride, pts, out_fm, diag_bias, low_bias,
                        far_col, state, palt, salt, salt2, rsc, filler=None):
        nqt = Tq // 128

        def qh(rbase, d2, qt):
            return qk_buf[rbase:rbase + 64,
                          qbuf_col + d2 * Tq + qt * 128: qbuf_col + d2 * Tq + (qt + 1) * 128]

        def kh(rbase, d2, c0, w):
            return qk_buf[rbase:rbase + 64, kbuf_col + d2 * Tq + c0: kbuf_col + d2 * Tq + c0 + w]

        def prow(r, c0, c1):
            if r == 0:
                return pbu[:, c0:c1]
            if r == 1:
                return palt[0][:, palt[1] + c0:palt[1] + c1]
            return pbu[:, 1024 + c0:1024 + c1]

        def srow(r, c0, c1):
            return (ssb, salt, salt2)[r][:, c0:c1]

        def do_chain(job):
            hs, qsub, qg, r, lix = job["hs"], job["qsub"], job["qg"], job["r"], job["lix"]
            h, rbase, d2, hh = job["head"]
            qt = 4 * qg + qsub
            klen = (qt + 1) * 128
            sc_nm = cols[:, r:r + 1]
            sc_l = cols[:, 32 + lix:33 + lix]
            b.need(v, state.get("srd" + str(r)))
            for kg in range(0, qt + 1, 4):
                w = min(512, klen - kg * 128)
                bank = pb[(kg // 4) % 2]
                b.need(t, state["qkv_tok"], bank.rd)
                mtk = b.inc(t.e.matmul(bank.t[:, 0:w], qh(rbase, d2, qt),
                                       kh(rbase, d2, kg * 128, w),
                                       start=True, stop=True), "tm")
                bank.wr = mtk
                b.need(v, mtk, state.get("brel_tok"), state.get("brel_low"))
                nt_in = w // 128
                nfar = 0
                while kg + nfar < qt - 1 and nfar < nt_in:
                    nfar += 1
                if nfar > 0:
                    if far_col is not None:
                        etk = v.e.tensor_scalar(
                            out=srow(r, kg * 128, (kg + nfar) * 128),
                            in0=bank.t[:, 0:nfar * 128], scalar1=far_col(h),
                            scalar2=None, op0=ALU.add)
                    else:
                        etk = v.e.tensor_copy(
                            out=srow(r, kg * 128, (kg + nfar) * 128),
                            in_=bank.t[:, 0:nfar * 128])
                for j in range(nfar, nt_in):
                    kt = kg + j
                    if kt == qt - 1 and low_bias is not None:
                        etk = v.e.tensor_tensor(
                            out=srow(r, kt * 128, (kt + 1) * 128),
                            in0=bank.t[:, j * 128:(j + 1) * 128],
                            in1=low_bias(h), op=ALU.add)
                    elif kt == qt:
                        etk = v.e.tensor_tensor(
                            out=srow(r, kt * 128, (kt + 1) * 128),
                            in0=bank.t[:, j * 128:(j + 1) * 128],
                            in1=diag_bias(h), op=ALU.add)
                    elif kt == qt - 1:
                        etk = v.e.tensor_copy(
                            out=srow(r, kt * 128, (kt + 1) * 128),
                            in_=bank.t[:, j * 128:(j + 1) * 128])
                bank.rd = b.inc(etk, "vm")
            b.need(v, b.tok("vm"), state.get("prd" + str(r)))
            mx = b.inc(v.e.tensor_scalar(out=prow(r, 0, klen),
                                         in0=srow(r, 0, klen), scalar1=-1.0,
                                         scalar2=None, op0=ALU.mult,
                                         op1=ALU.min, accum_out=sc_nm), "vm")
            b.need(s, mx)
            ex = b.inc(s.e.activation(out=prow(r, 0, klen), in_=srow(r, 0, klen),
                                      func=AF.Exp, bias=sc_nm, scale=1.0,
                                      accum_out=sc_l), "sm")
            state["srd" + str(r)] = ex
            job["ex"] = ex

        def do_transposes(job):
            hs, qsub, qg, r = job["hs"], job["qsub"], job["qg"], job["r"]
            qt = 4 * qg + qsub
            ptt, ptc, ptp = job["pt"]
            for kt0 in range(0, qt + 1, 4):
                nb = min(4, qt + 1 - kt0)
                bank = pb[4 + (kt0 // 4) % 2]
                b.need(t, job["ex"], bank.rd, consts_tok)
                for j in range(nb):
                    ins = t.e.matmul(bank.t[:, ts(j)],
                                     prow(r, (kt0 + j) * 128, (kt0 + j + 1) * 128),
                                     sident[:, :], start=True, stop=True)
                bank.wr = b.inc(ins, "tm")
                eng2 = s if (kt0 // 4) % 2 == 0 else v
                b.need(eng2, bank.wr)
                dst = bass.AP(ptt, ptc + kt0 * 512 + qsub * 128,
                              [[ptp, 128], [512, nb], [1, 128]])
                if eng2 is s:
                    ctk = b.inc(s.e.copy(out=dst, in_=bank.t[:, 0:nb * 128]), "sm")
                else:
                    ctk = b.inc(v.e.tensor_copy(out=dst, in_=bank.t[:, 0:nb * 128]), "vm")
                bank.rd = ctk
            state["prd" + str(r)] = b.tok("tm")

        def do_pv(head, hi, qg, pt, rbase_cols):
            h, rbase, d2, hh = head
            ktmax = 4 * qg + 3
            ptt, ptc, ptp = pt
            rtok = b.tok("vm")
            bank7 = pb[7]
            b.need(t, rtok, bank7.rd, consts_tok)
            for qsub in range(4):
                ins = t.e.matmul(bank7.t[0:1, qsub * 128:(qsub + 1) * 128],
                                 cols[:, rbase_cols + qsub:rbase_cols + qsub + 1],
                                 fident[:, :], start=True, stop=True)
            rt1 = b.inc(ins, "tm")
            state["rcols_rd"] = rt1
            bank7.wr = rt1
            b.need(s, rt1)
            rs1 = b.inc(s.e.copy(out=rsc[0:1, 0:512], in_=bank7.t[0:1, 0:512]), "sm")
            bank7.rd = rs1
            b.need(t, rs1)
            rt2 = b.inc(t.e.matmul(bank7.t[:, :], sones[0:1, 0:128],
                                   rsc[0:1, 0:512], start=True, stop=True), "tm")
            bank7.wr = rt2
            b.need(s, rt2, state.get("rsc_rd"))
            rs2 = b.inc(s.e.copy(out=rsc[:, 512:1024], in_=bank7.t[:, :]), "sm")
            bank7.rd = rs2
            bank = pb[2 + (hi % 2)]
            b.need(t, b.tok("sm"), b.tok("vm"), bank.rd)
            for kt in range(ktmax + 1):
                off = max(0, kt - 4 * qg) * 128
                ins = t.e.matmul(bank.t[rbase:rbase + 64, off:512],
                                 v_buf[:, vbuf_col + kt * vstride + hh * 64:
                                       vbuf_col + kt * vstride + hh * 64 + 64],
                                 ptt[:, ptc + kt * 512 + off:ptc + (kt + 1) * 512],
                                 start=(kt == 0), stop=(kt == ktmax))
            mtk = b.inc(ins, "tm")
            bank.wr = mtk
            b.need(v, mtk, rs2)
            etk = b.inc(v.e.tensor_mul(out=out_fm(h, qg)[rbase:rbase + 64, :],
                                       in0=rsc[rbase:rbase + 64, 512:1024],
                                       in1=bank.t[rbase:rbase + 64, :]), "vm")
            bank.rd = etk
            state["rsc_rd"] = etk

        rown = [0]

        def next_row(maxrows):
            r = rown[0] % maxrows
            rown[0] += 1
            return r

        for p0 in range(0, len(heads), 2):
            pair = heads[p0:p0 + 2]
            for qg in range(nqt // 4):
                klen_max = (4 * qg + 4) * 128
                paired = PAIR_ATTN and len(pair) == 2 and klen_max <= 1024
                if paired:
                    jobs = []
                    for qsub in range(4):
                        for hs, head in enumerate(pair):
                            jobs.append({"hs": hs, "head": head, "qsub": qsub,
                                         "qg": qg, "r": next_row(3),
                                         "lix": qsub * 2 + hs, "pt": pts[hs]})
                    pend = []
                    for job in jobs:
                        do_chain(job)
                        if filler:
                            filler.pop(0)()
                        if len(pend) >= 2:
                            do_transposes(pend.pop(0))
                        pend.append(job)
                    for job in pend:
                        do_transposes(job)
                    b.need(v, jobs[-1]["ex"], state.get("rcols_rd"))
                    for job in jobs:
                        b.inc(v.e.reciprocal(
                            out=cols[:, 16 + job["hs"] * 4 + job["qsub"]:
                                     17 + job["hs"] * 4 + job["qsub"]],
                            in_=cols[:, 32 + job["lix"]:33 + job["lix"]]), "vm")
                    for hs, head in enumerate(pair):
                        do_pv(head, p0 + hs, qg, pts[hs], 16 + hs * 4)
                else:
                    for hs, head in enumerate(pair):
                        maxr = 3 if klen_max <= 1024 else 2
                        jobs = []
                        for qsub in range(4):
                            jobs.append({"hs": hs, "head": head, "qsub": qsub,
                                         "qg": qg, "r": next_row(maxr),
                                         "lix": qsub, "pt": pts[0]})
                        pend = []
                        for job in jobs:
                            do_chain(job)
                            if filler:
                                filler.pop(0)()
                            if len(pend) >= 1:
                                do_transposes(pend.pop(0))
                            pend.append(job)
                        for job in pend:
                            do_transposes(job)
                        b.need(v, jobs[-1]["ex"], state.get("rcols_rd"))
                        for job in jobs:
                            b.inc(v.e.reciprocal(
                                out=cols[:, 16 + job["qsub"]:17 + job["qsub"]],
                                in_=cols[:, 32 + job["lix"]:33 + job["lix"]]), "vm")
                        do_pv(head, p0 + hs, qg, pts[0], 16)
        while filler:
            filler.pop(0)()

    def block_l1a(seq, final):
        b.dma_barrier(q, g)
        tls = ep_state.get("tiles", {})
        fallback = b.tok("vm")
        xftok = make_xfm(lambda ti: tls.get(ti, fallback))
        st = {"qkv_tok": None, "brel_tok": None}
        pre_rd = b.tok("tm")
        dtk = None
        for h in range(16):
            src = bass.AP(scr["erev"].tensor, h * 400 + 129, [[1, 128], [1, 128]])
            b.dma(g, bigB[:, 12288 + h * 128:12288 + (h + 1) * 128], src,
                  "gbl", waits=(pre_rd,) if h == 0 else ())
            srcd = bass.AP(scr["erev"].tensor, h * 400 + 1, [[1, 128], [1, 128]])
            dtk = b.dma(g, pbu[:, h * 128:(h + 1) * 128], srcd, "gbd",
                        waits=(pre_rd,) if h == 0 else ())
        btk = None
        for h in range(16):
            b.need(v, dtk, consts_tok)
            rap = bass.AP(pbu, h * 128 + 127, [[2048, 128], [-1, 128]])
            btk = b.inc(v.e.tensor_tensor(out=bigB[:, 14336 + h * 128:14336 + (h + 1) * 128],
                                          in0=smask[:, :], in1=rap, op=ALU.add), "vm")
        st["brel_tok"] = btk
        st["brel_low"] = b.tok("gbl")
        st["pbu_rd"] = btk

        def diag_bias(h):
            return bigB[:, 14336 + h * 128:14336 + (h + 1) * 128]

        def low_bias(h):
            return bass.AP(bigB, 12288 + h * 128 + 127, [[16384, 128], [-1, 128]])

        def far_col(h):
            return e0c[:, h:h + 1]

        def out_fm(h, qg):
            hp = h // 2
            return bigA[:, hp * 2048 + qg * 512:hp * 2048 + (qg + 1) * 512]

        def load_qkv(g2):
            base = (g2 % 2) * 3072
            toks = []
            for j, col0 in enumerate((g2 * 128, 1024 + g2 * 128, 2048 + g2 * 128)):
                dst = bass.AP(wbuf, base + j * 1024, [[16384, 128], [128, 8], [1, 128]])
                srcw = bass.AP(W["L1_attn_Wqkv"].tensor, col0,
                               [[3072, 128], [3072 * 128, 8], [1, 128]])
                toks.append(b.dma(g, dst, srcw, "gw",
                                  waits=(b.tok("tm"),) if j == 0 else ()))
            rbs = 1280 + (g2 % 2) * 384
            btk = None
            for j, col0 in enumerate((g2 * 128, 1024 + g2 * 128, 2048 + g2 * 128)):
                btk = b.dma(q, smal[0:1, rbs + j * 128:rbs + (j + 1) * 128],
                            W["L1_attn_bqkv"][0:1, col0:col0 + 128], "qr",
                            waits=(b.tok("tm"), b.tok("qr")) if j == 0 else ())
            return toks[-1], btk, rbs

        def qkv_fillers(g2, wtok, btk, rbs):
            base = (g2 % 2) * 3072
            bufb = (g2 % 2) * 6144
            fill = []
            for j, scale in ((0, True), (1, False)):
                for tg in range(4):
                    def go(j=j, tg=tg, scale=scale):
                        bank = pb[6]
                        b.need(t, wtok, btk, xftok, bank.rd)
                        for c in range(NCH):
                            t.e.matmul(bank.t[:, :],
                                       wbuf[:, base + j * 1024 + c * 128:
                                            base + j * 1024 + (c + 1) * 128],
                                       xsl(c, tg), start=(c == 0), stop=False)
                        ins = t.e.matmul(bank.t[:, :],
                                         smal[0:1, rbs + j * 128:rbs + (j + 1) * 128],
                                         sones[0:1, 0:512], start=False, stop=True)
                        mtk = b.inc(ins, "tm")
                        bank.wr = mtk
                        b.need(v, mtk)
                        dstc = bigB[:, bufb + j * 2048 + tg * 512:
                                    bufb + j * 2048 + (tg + 1) * 512]
                        if scale:
                            etk = b.inc(v.e.tensor_scalar(out=dstc, in0=bank.t[:, :],
                                                          scalar1=0.125, scalar2=None,
                                                          op0=ALU.mult), "vm")
                        else:
                            etk = b.inc(v.e.tensor_copy(out=dstc, in_=bank.t[:, :]), "vm")
                        bank.rd = etk
                    fill.append(go)
            for ti in range(NT):
                def gov(ti=ti):
                    bank = pb[6]
                    b.need(t, wtok, btk, xftok, bank.rd)
                    for c in range(NCH):
                        t.e.matmul(bank.t[:, 0:128], xslt(c, ti),
                                   wbuf[:, base + 2048 + c * 128:base + 2048 + (c + 1) * 128],
                                   start=(c == 0), stop=False)
                    ins = t.e.matmul(bank.t[:, 0:128], sones[0:1, 0:128],
                                     smal[0:1, rbs + 256:rbs + 384],
                                     start=False, stop=True)
                    mtk = b.inc(ins, "tm")
                    bank.wr = mtk
                    b.need(v, mtk)
                    etk = b.inc(v.e.tensor_copy(
                        out=bigB[:, bufb + 4096 + ti * 128:bufb + 4096 + (ti + 1) * 128],
                        in_=bank.t[:, 0:128]), "vm")
                    bank.rd = etk
                fill.append(gov)
            return fill

        w0, b0, rb0 = load_qkv(0)
        f0 = qkv_fillers(0, w0, b0, rb0)
        for go in f0:
            go()
        for g2 in range(8):
            st["qkv_tok"] = b.tok("vm")
            if g2 < 7:
                wn, bn, rbn = load_qkv(g2 + 1)
                fillers = qkv_fillers(g2 + 1, wn, bn, rbn)
            else:
                fillers = []
            bufb = (g2 % 2) * 6144
            heads = [(g2 * 2 + j, 64 * j, 0, j) for j in range(2)]
            attn_softmax_pv(seq, heads, T, bigB, bufb, bufb + 2048, bigB,
                            bufb + 4096, 128,
                            [(bigC, 0, 8192), (bigC, 4096, 8192)],
                            out_fm, diag_bias, low_bias, far_col, st,
                            (wbuf, 6144), wbuf[:, 8192:12288].bitcast(F32),
                            wbuf[:, 14336:16384].bitcast(F32),
                            wbuf[:, 12288:14336].bitcast(F32), filler=fillers)
        attn_tok = b.tok("vm")
        wo_rd = b.tok("tm")
        dst = bass.AP(wbuf, 0, [[16384, 128], [1024, 8], [1, 1024]])
        src = bass.AP(W["L1_attn_Wout"].tensor, 0, [[1024, 128], [1024 * 128, 8], [1, 1024]])
        wotk = b.dma(g, dst, src, "go", waits=(wo_rd, b.tok("vm"), b.tok("sm")))
        botk = b.dma(q, ssb[0:1, 1024:2048], W["L1_attn_bout"][0:1, :], "qr",
                     waits=(st.get("srd0"), st.get("srd1"), st.get("srd2"),
                            b.tok("qr")))
        for ti in range(NT):
            for ch in range(2):
                bank = pb[4 + ch]
                b.need(t, wotk, botk, attn_tok, bank.rd)
                for d in range(NCH):
                    t.e.matmul(bank.t[:, :],
                               bigA[:, d * 2048 + ti * 128:d * 2048 + (ti + 1) * 128],
                               wbuf[:, d * 1024 + ch * 512:d * 1024 + (ch + 1) * 512],
                               start=(d == 0), stop=False)
                ins = t.e.matmul(bank.t[:, :], sones[0:1, 0:128],
                                 ssb[0:1, 1024 + ch * 512:1024 + (ch + 1) * 512],
                                 start=False, stop=True)
                bank.wr = b.inc(ins, "tm")
            ln_epilogue(seq, ti, (pb[4].t[:, :], pb[4].wr), (pb[5].t[:, :], pb[5].wr),
                        final, banks=(pb[4], pb[5]))

    def block_l1f(seq, final):
        b.dma_barrier(q, g)
        tls = ep_state.get("tiles", {})
        fallback = b.tok("vm")
        xftok = make_xfm(lambda ti: tls.get(ti, fallback))
        sc_tok = None
        for i, sn in enumerate(("L1_ffn_w1_s", "L1_ffn_w2_s",
                                "L1_ffn_h1_s", "L1_ffn_h2_s")):
            stk = b.dma(q, ssb[0:32, 1024:1152],
                        bass.AP(W[sn].tensor, 0, [[128, 32], [1, 128]]),
                        "qsc", waits=(sc_tok or b.tok("tm"), b.tok("qsc")))
            bankx = pb[4 + (i % 2)]
            b.need(t, stk, bankx.rd, consts_tok)
            ttk = b.inc(t.e.transpose(bankx.t[:, 0:32], ssb[0:32, 1024:1152],
                                      fident[0:32, 0:32]), "tm")
            bankx.wr = ttk
            b.need(s, ttk)
            sc_tok = b.inc(s.e.copy(out=smal[:, 1024 + i * 32:1024 + (i + 1) * 32],
                                    in_=bankx.t[:, 0:32]), "sm")
            bankx.rd = sc_tok
        scl_tok = sc_tok
        slot_rd = [None] * 4
        cur = [0]

        def wload(src, width):
            i = cur[0] % 4
            cur[0] += 1
            sl = wbuf[:, i * 4096:i * 4096 + width]
            tk = b.dma(q, sl, src, "qs" + str(i), waits=(slot_rd[i],))
            return i, sl, tk

        for tg in range(4):
            for half, snm in ((0, "f1"), (1, "f2")):
                for f4 in range(8):
                    i, sl, wtk = wload(
                        bass.AP(scr[snm].tensor, f4 * 512 * 1024,
                                [[1024, 128], [128 * 1024, 4], [1, 1024]]), 4096)
                    for fj in range(4):
                        f = f4 * 4 + fj
                        bank = pb[f % 2]
                        b.need(t, wtk, xftok, bank.rd)
                        for c in range(NCH):
                            ins = t.e.matmul(bank.t[:, :], sl[:, fj * 1024 + c * 128:fj * 1024 + (c + 1) * 128],
                                             xsl(c, tg), start=(c == 0), stop=(c == NCH - 1))
                        mtk = b.inc(ins, "tm")
                        slot_rd[i] = mtk
                        bank.wr = mtk
                        dstc = bigA[:, f * 512:(f + 1) * 512]
                        scol = smal[:, 1024 + half * 32 + f:1025 + half * 32 + f]
                        if half == 0:
                            b.need(s, mtk, scl_tok)
                            etk = b.inc(s.e.activation(out=dstc, in_=bank.t[:, :],
                                                       func=AF.Silu, scale=scol), "sm")
                        else:
                            b.need(v, mtk, b.tok("sm"))
                            etk = b.inc(v.e.scalar_tensor_tensor(
                                out=dstc, in0=bank.t[:, :], scalar=scol, in1=dstc,
                                op0=ALU.mult, op1=ALU.mult), "vm")
                        bank.rd = etk
            for hn, srcb, dstb in (("h1", bigA, bigB), ("h2", bigB, bigA)):
                src_ready = (b.tok("vm"), b.tok("sm"))
                for f in range(32):
                    i, sl, wtk = wload(scr[hn][ts(f), :], 4096)
                    bank = pb[f % 2]
                    b.need(t, wtk, bank.rd, *(src_ready if f == 0 else ()))
                    for c in range(32):
                        ins = t.e.matmul(bank.t[:, :], sl[:, ts(c)],
                                         srcb[:, c * 512:(c + 1) * 512],
                                         start=(c == 0), stop=(c == 31))
                    mtk = b.inc(ins, "tm")
                    slot_rd[i] = mtk
                    bank.wr = mtk
                    b.need(s, mtk)
                    hcol = smal[:, 1024 + (64 if hn == "h1" else 96) + f:
                                1025 + (64 if hn == "h1" else 96) + f]
                    etk = b.inc(s.e.activation(out=dstb[:, f * 512:(f + 1) * 512],
                                               in_=bank.t[:, :], func=AF.Gelu,
                                               scale=hcol), "sm")
                    bank.rd = etk
            a2_ready = b.tok("sm")
            hold = [None] * 4
            for chalf in range(2):
                htok = None
                w3_waits = [slot_rd[0], slot_rd[1], slot_rd[2], slot_rd[3],
                            b.tok("tm") if chalf == 1 else None]
                gtoks = [None] * 4
                for gr in range(4):
                    for fj in range(8):
                        f = gr * 8 + fj
                        sl = wbuf[:, f * 512:(f + 1) * 512]
                        gtoks[gr] = b.dma(q, sl,
                                          scr["f3"][ts(f), chalf * 512:(chalf + 1) * 512],
                                          "qw3" + str(gr),
                                          waits=tuple(w3_waits) if f == 0 else ())
                for tt in range(4):
                    ti = tg * 4 + tt
                    bank = pb[tt] if chalf == 0 else pb[4 + (tt % 2)]
                    b.need(t, a2_ready, bank.rd)
                    for f in range(32):
                        b.need(t, gtoks[f // 8])
                        ins = t.e.matmul(bank.t[:, :],
                                         bigA[:, f * 512 + tt * 128: f * 512 + (tt + 1) * 128],
                                         wbuf[:, f * 512:(f + 1) * 512],
                                         start=(f == 0), stop=(f == 31))
                    bank.wr = b.inc(ins, "tm")
                    if chalf == 0:
                        hold[tt] = bank
                    else:
                        ln_epilogue(seq, ti, (hold[tt].t[:, 0:512], hold[tt].wr),
                                    (bank.t[:, 0:512], bank.wr), final,
                                    banks=(hold[tt], bank))
                wtm = b.tok("tm")
                for i in range(4):
                    slot_rd[i] = wtm
        return

    prepass()
    pp_toks = (b.tok("dgP"), b.tok("dgB"), b.tok("dgN0"), b.tok("dgN1"),
               b.tok("dgC"), b.tok("dgC3"), b.tok("dgA0"), b.tok("dgA1"), b.tok("vm"))
    b.need(q, *pp_toks)
    b.need(g, *pp_toks)
    for seq in range(n_seq):
        ltok = load_x(seq)
        for bi, blk in enumerate(blocks):
            final = (bi == len(blocks) - 1)
            if bi == 0:
                b.need(t, ltok)
                b.need(v, ltok)
            if blk == "L1F":
                block_l1f(seq, final)
            elif blk == "L0F":
                block_l0f(seq, final)
            elif blk == "L1A":
                block_l1a(seq, final)
            elif blk == "L0A":
                block_l0a(seq, final)
            else:
                raise NotImplementedError(blk)
    b.need(q, b.tok("dy"))
    b.need(g, b.tok("dy"))
    stack.close()
    return nc, in_names



def _consts():
    ident = np.eye(128, dtype=np.float32)
    mask = np.where(np.arange(128)[None, :] > np.arange(128)[:, None],
                    np.float32(NEG), np.float32(0.0)).astype(np.float32)
    return {"c_ident": ident, "c_mask": mask}


def _prep_inputs(inputs):
    out = {}
    for k, vv in inputs.items():
        a = np.asarray(vv)
        if a.ndim == 1:
            a = a[None, :]
        elif a.ndim == 3:
            a = a.reshape(a.shape[0] * a.shape[1], a.shape[2])
        out[k] = np.ascontiguousarray(a, dtype=np.float32)
    return out


def run_blocks(inputs, blocks=ALL_BLOCKS, x_override=None):
    xs = np.asarray(x_override if x_override is not None else inputs["L0_attn_x"],
                    dtype=np.float32)
    winp = _prep_inputs({k: v for k, v in inputs.items() if k != "L0_attn_x"})
    consts = _consts()
    nc, in_names = build_program(blocks)
    in_maps = []
    for c in range(NCORES):
        m = {"x": np.ascontiguousarray(xs[c * SEQN:(c + 1) * SEQN].reshape(SEQN * T, D))}
        for n in in_names:
            if n == "x":
                continue
            m[n] = consts[n] if n in consts else winp[n]
        in_maps.append(m)
    res = run_bass_kernel_spmd(nc, in_maps, core_ids=list(range(NCORES)))
    return np.concatenate([r["y"].reshape(SEQN, T, D) for r in res.results], axis=0)


def kernel(**inputs):
    return run_blocks(inputs, ALL_BLOCKS)


# version 13
# speedup vs baseline: 1.0060x; 1.0060x over previous
import contextlib
import numpy as np
import ml_dtypes
import concourse.bass as bass
import concourse.mybir as mybir
from concourse.bass_utils import run_bass_kernel_spmd

F32 = mybir.dt.float32
BF16 = mybir.dt.bfloat16
AF = mybir.ActivationFunctionType
ALU = mybir.AluOpType
AX = mybir.AxisListType
bfdt = ml_dtypes.bfloat16

BATCH, T, D = 32, 2048, 1024
NCORES = 8
SEQN = BATCH // NCORES
NT = T // 128
NCH = D // 128
NEG = -1.0e30

ALL_BLOCKS = ("L0A", "L0F", "L1A", "L1F")
PAIR_ATTN = False


def ts(i, n=128):
    return slice(i * n, (i + 1) * n)


class Eng:
    def __init__(self, e, name):
        self.e = e
        self.name = name
        self.last_wait = {}


class Bld:
    def __init__(self, nc, stack):
        self.nc = nc
        self.stack = stack
        self.t = Eng(nc.tensor, "t")
        self.v = Eng(nc.vector, "v")
        self.s = Eng(nc.scalar, "s")
        self.q = Eng(nc.sync, "q")
        self.g = Eng(nc.gpsimd, "g")
        self.sems = {}
        self.cnt = {}

    def sem(self, name):
        if name not in self.sems:
            self.sems[name] = self.stack.enter_context(self.nc.semaphore(name))
            self.cnt[name] = 0
        return self.sems[name]

    def inc(self, ins, name, k=1):
        ins.then_inc(self.sem(name), k)
        self.cnt[name] += k
        return (name, self.cnt[name])

    def tok(self, name):
        self.sem(name)
        return (name, self.cnt[name])

    def need(self, eng, *tokens):
        best = {}
        for tk in tokens:
            if tk is None:
                continue
            name, val = tk
            if val > best.get(name, 0):
                best[name] = val
        for name, val in best.items():
            if eng.last_wait.get(name, -1) >= val:
                continue
            eng.e.wait_ge(self.sem(name), val)
            eng.last_wait[name] = val

    def dma_barrier(self, *engs, light=False):
        skip = ("vm", "sm") if light else ()
        toks = [(n, c) for n, c in self.cnt.items() if c > 0 and n not in skip]
        for e in engs:
            self.need(e, *toks)

    def dma(self, eng, out, in_, sname, waits=(), **kw):
        self.need(eng, *waits)
        ins = eng.e.dma_start(out=out, in_=in_, **kw)
        return self.inc(ins, sname, 16)


class Bank:
    __slots__ = ("t", "name", "wr", "rd")

    def __init__(self, tens, name):
        self.t = tens
        self.name = name
        self.wr = None
        self.rd = None



def build_program(blocks, n_seq=SEQN):
    nc = bass.Bass("TRN2", target_bir_lowering=False)
    stack = contextlib.ExitStack()

    x_in = nc.dram_tensor("x", [n_seq * T, D], F32, kind="ExternalInput").ap()
    y_out = nc.dram_tensor("y", [n_seq * T, D], F32, kind="ExternalOutput").ap()
    in_names = ["x"]

    def din(name, shape):
        in_names.append(name)
        return nc.dram_tensor(name, shape, F32, kind="ExternalInput").ap()

    W = {}
    shapes = {
        "L0A": [("L0_attn_w_lqkv", [512, 1536]), ("L0_attn_b_lqkv", [1, 1536]),
                ("L0_attn_w_gqkv", [512, 1536]), ("L0_attn_b_gqkv", [1, 1536]),
                ("L0_attn_w_comp", [4096, 1024]), ("L0_attn_b_comp", [1, 1024]),
                ("L0_attn_w_lproj", [512, 512]), ("L0_attn_b_lproj", [1, 512]),
                ("L0_attn_w_gproj", [512, 512]), ("L0_attn_b_gproj", [1, 512]),
                ("L0_attn_w_gate", [1024, 2]), ("L0_attn_b_gate", [1, 2])],
        "L0F": [("L0_ffn_ln_gamma", [1, 1024]), ("L0_ffn_ln_beta", [1, 1024]),
                ("L0_ffn_gate_w", [8, 1024]), ("L0_ffn_w1", [8 * 2048, 1024]),
                ("L0_ffn_w2", [8 * 1024, 2048])],
        "L1A": [("L1_attn_Wqkv", [1024, 3072]), ("L1_attn_bqkv", [1, 3072]),
                ("L1_attn_Wout", [1024, 1024]), ("L1_attn_bout", [1, 1024]),
                ("L1_attn_rel_emb", [257, 16])],
        "L1F": [("L1_ffn_w1_w", [4096, 1024]), ("L1_ffn_w1_s", [1, 4096]),
                ("L1_ffn_w2_w", [4096, 1024]), ("L1_ffn_w2_s", [1, 4096]),
                ("L1_ffn_h1_w", [4096, 4096]), ("L1_ffn_h1_s", [1, 4096]),
                ("L1_ffn_h2_w", [4096, 4096]), ("L1_ffn_h2_s", [1, 4096]),
                ("L1_ffn_w3_w", [1024, 4096]), ("L1_ffn_w3_s", [1, 1024])],
    }
    for blk in blocks:
        for n, sh in shapes[blk]:
            W[n] = din(n, sh)

    c_ident = din("c_ident", [128, 128])
    c_mask = din("c_mask", [128, 128])

    scr = {}
    stg = {}
    if "L0F" in blocks:
        stg["m1"] = nc.dram_tensor("g_m1", [8 * 2048, 1024], BF16).ap()
        stg["m2"] = nc.dram_tensor("g_m2", [8 * 1024, 2048], BF16).ap()
        scr["m1"] = nc.dram_tensor("s_m1", [8 * 2048, 1024], BF16).ap()
        scr["m2"] = nc.dram_tensor("s_m2", [8 * 2048, 1024], BF16).ap()
    if "L1F" in blocks:
        stg["f1"] = nc.dram_tensor("g_f1", [4096, 1024], BF16).ap()
        stg["f2"] = nc.dram_tensor("g_f2", [4096, 1024], BF16).ap()
        stg["h1"] = nc.dram_tensor("g_h1", [4096, 4096], BF16).ap()
        stg["h2"] = nc.dram_tensor("g_h2", [4096, 4096], BF16).ap()
        stg["f3"] = nc.dram_tensor("g_f3", [1024, 4096], BF16).ap()
        scr["f1"] = nc.dram_tensor("s_f1", [4096, 1024], BF16).ap()
        scr["f2"] = nc.dram_tensor("s_f2", [4096, 1024], BF16).ap()
        scr["h1"] = nc.dram_tensor("s_h1", [4096, 4096], BF16).ap()
        scr["h2"] = nc.dram_tensor("s_h2", [4096, 4096], BF16).ap()
        scr["f3"] = nc.dram_tensor("s_f3", [4096, 1024], BF16).ap()
    if "L1A" in blocks:
        scr["erev"] = nc.dram_tensor("s_erev", [16, 400], F32).ap()

    b = Bld(nc, stack)
    t, v, s, q, g = b.t, b.v, b.s, b.q, b.g

    def sbuf(name, shape, dt):
        return stack.enter_context(nc.sbuf_tensor(name, shape, dt))

    resid = sbuf("resid", [128, NT * 1024], BF16)
    xfm = sbuf("xfm", [128, NCH * T], BF16)
    bigA = sbuf("bigA", [128, 16384], BF16)
    bigB = sbuf("bigB", [128, 16384], BF16)
    bigC = sbuf("bigC", [128, 8192], BF16)
    wbuf = sbuf("wbuf", [128, 16384], BF16)
    ssb = sbuf("ssb", [128, 2048], F32)
    pbu = sbuf("pbu", [128, 2048], BF16)
    smal = sbuf("smal", [128, 2048], F32)
    cols = sbuf("cols", [128, 64], F32)
    crow = sbuf("crow", [16, 512], F32)
    cpool = sbuf("cpool", [128, 8], F32)
    sident = sbuf("sident", [128, 128], BF16)
    smask = sbuf("smask", [128, 128], F32)
    sones = sbuf("sones", [1, 512], F32)

    pb = [Bank(stack.enter_context(nc.psum_tensor(f"pb{i}", [128, 512], F32)), f"pb{i}")
          for i in range(8)]
    fident = sbuf("fident", [128, 128], F32)
    gwp = sbuf("gwp", [128, 64], BF16)
    s23 = sbuf("s23", [128, 16], F32)
    e0c = sbuf("e0c", [128, 16], F32)

    CP_EPS, CP_QS, CP_NEG1 = 0, 1, 2

    mtk0 = b.dma(q, smask[:, :], c_mask[:, :], "dqP")
    itk0 = b.dma(q, ssb[:, 0:128], c_ident[:, :], "dqP")
    v.e.memset(cpool[:, CP_EPS:CP_EPS + 1], 1e-5)
    v.e.memset(cpool[:, CP_QS:CP_QS + 1], 0.125)
    v.e.memset(cpool[:, CP_NEG1:CP_NEG1 + 1], -1.0)
    v.e.memset(sones[:, :], 1.0)
    b.need(v, itk0, mtk0)
    v.e.tensor_copy(out=sident[:, :], in_=ssb[:, 0:128])
    consts_tok = b.inc(v.e.tensor_copy(out=fident[:, :], in_=ssb[:, 0:128]), "vm")

    rp_state = {}

    def repack(wsrc, dst, nout, nin, kind, srow=None, src_row0=0, dst_row0=0,
               scol_ix=16, cast_tok=None, scol_sem="dgC3"):
        nci, nco = nin // 128, nout // 128
        assert nin <= 4096
        nats = [wbuf[:, 0:nin], wbuf[:, 4096:4096 + nin]]
        asms = [wbuf[:, 8192:8192 + nin], wbuf[:, 12288:12288 + nin]]
        nat_rd = [rp_state.get("n0"), rp_state.get("n1")]
        asm_rd = [rp_state.get("a0"), rp_state.get("a1")]
        scol_rd = b.tok("vm")
        scol = cols[:, scol_ix:scol_ix + 1]
        for f in range(nco):
            par = f % 2
            nat, asm = nats[par], asms[par]
            ntk = b.dma(g, nat,
                        wsrc[src_row0 + f * 128: src_row0 + (f + 1) * 128, :],
                        "dgN" + str(par), waits=(nat_rd[par],))
            if srow is not None:
                ctk = b.dma(g, scol,
                            srow[0:1, ts(f)].rearrange("a (p c) -> (a p) c", c=1),
                            scol_sem, waits=(scol_rd,), allow_slow_non_contiguous=True)
                b.need(v, ntk, ctk)
                ntk = b.inc(v.e.tensor_scalar(out=nat, in0=nat, scalar1=scol,
                                              scalar2=None, op0=ALU.mult), "vm")
                scol_rd = ntk
            asm_wr = None
            for cg in range(0, nci, 4):
                bank = pb[(cg // 4) % 4]
                b.need(t, ntk, bank.rd, consts_tok)
                nb = min(4, nci - cg)
                for j in range(nb):
                    ins = t.e.matmul(bank.t[:, ts(j)], nat[:, ts(cg + j)],
                                     sident[:, :], start=True, stop=True)
                bank.wr = b.inc(ins, "tm")
                b.need(v, bank.wr, asm_rd[par])
                cp = v.e.tensor_copy(out=asm[:, cg * 128:(cg + nb) * 128],
                                     in_=bank.t[:, 0:nb * 128])
                asm_wr = b.inc(cp, "vm")
                bank.rd = asm_wr
            nat_rd[par] = b.tok("tm")
            if kind == "img":
                asm_rd[par] = b.dma(q, dst[dst_row0 + f * 128: dst_row0 + (f + 1) * 128, :],
                                    asm, "dgA" + str(par), waits=(asm_wr,))
            else:
                for c in range(nci):
                    tk = b.dma(q, dst[dst_row0 + c * 128: dst_row0 + (c + 1) * 128, ts(f)],
                               asm[:, ts(c)], "dgA" + str(par),
                               waits=(asm_wr,) if c == 0 else ())
                asm_rd[par] = tk
        rp_state["n0"], rp_state["n1"] = nat_rd
        rp_state["a0"], rp_state["a1"] = asm_rd

    def prepass():
        if "L1F" in blocks:
            repack(W["L1_ffn_w1_w"], scr["f1"], 4096, 1024, "img")
            repack(W["L1_ffn_w2_w"], scr["f2"], 4096, 1024, "img")
            repack(W["L1_ffn_h1_w"], scr["h1"], 4096, 4096, "img")
            repack(W["L1_ffn_h2_w"], scr["h2"], 4096, 4096, "img")
            repack(W["L1_ffn_w3_w"], scr["f3"], 1024, 4096, "str",
                   srow=W["L1_ffn_w3_s"])
        if "L0F" in blocks:
            for e in range(8):
                repack(W["L0_ffn_w1"], scr["m1"], 2048, 1024, "img",
                       src_row0=e * 2048, dst_row0=e * 2048)
                repack(W["L0_ffn_w2"], scr["m2"], 1024, 2048, "str",
                       src_row0=e * 1024, dst_row0=e * 2048)
            prep_router()
        if "L1A" in blocks:
            prep_relemb()

    def prep_relemb():
        r0 = ssb[:, 1024:1072]
        tk0 = b.dma(q, r0[:, 0:16], W["L1_attn_rel_emb"][0:128, :], "dqE",
                    waits=(consts_tok,))
        tk1 = b.dma(q, r0[:, 16:32], W["L1_attn_rel_emb"][128:256, :], "dqE")
        tk2 = b.dma(q, r0[0:1, 32:48], W["L1_attn_rel_emb"][256:257, :], "dqE")
        b.need(t, tk0, tk2, consts_tok)
        e0tk = b.inc(t.e.matmul(pb[2].t[:, 0:16], sones[0:1, 0:128], r0[0:1, 0:16],
                                start=True, stop=True), "tm")
        pb[2].wr = e0tk
        b.need(v, e0tk)
        e0cp = b.inc(v.e.tensor_copy(out=e0c[:, :], in_=pb[2].t[:, 0:16]), "vm")
        pb[2].rd = e0cp
        b.need(t, tk1, e0cp, pb[4].rd, pb[5].rd)
        tr0 = b.inc(t.e.transpose(pb[4].t[0:16, 0:128], r0[:, 0:16], fident[:, :]), "tm")
        tr1 = b.inc(t.e.transpose(pb[5].t[0:16, 0:128], r0[:, 16:32], fident[:, :]), "tm")
        tr2 = b.inc(t.e.transpose(pb[2].t[0:16, 0:1], r0[0:1, 32:48], fident[0:1, 0:1]), "tm")
        pb[4].wr = pb[5].wr = pb[2].wr = tr2
        b.need(v, tr0, tr1, tr2)
        v.e.tensor_copy(out=crow[0:16, 0:128], in_=pb[4].t[0:16, 0:128])
        v.e.tensor_copy(out=crow[0:16, 128:256], in_=pb[5].t[0:16, 0:128])
        cpk = b.inc(v.e.tensor_copy(out=crow[0:16, 256:257], in_=pb[2].t[0:16, 0:1]), "vm")
        pb[4].rd = pb[5].rd = pb[2].rd = cpk
        b.need(v, cpk)
        rev = crow[0:16, 257:512]
        w1 = b.inc(v.e.tensor_copy(out=smal[0:16, 0:257],
                                   in_=bass.AP(crow, 256, [[512, 16], [-1, 257]])), "vm")
        b.need(v, w1)
        w2 = b.inc(v.e.tensor_scalar(out=smal[0:16, 257:385], in0=smal[0:16, 0:128],
                                     scalar1=0.0, scalar2=crow[0:16, 0:1],
                                     op0=ALU.mult, op1=ALU.add), "vm")
        b.dma(g, scr["erev"][0:16, 0:385], smal[0:16, 0:385], "dgP", waits=(w2,))

    def prep_router():
        gw = ssb[0:8, 0:1024]
        gtk = b.dma(q, gw, W["L0_ffn_gate_w"][:, :], "dqR", waits=(consts_tok,))
        v.e.memset(cpool[:, 3:4], 1.0)
        sc_g = cols[:, 24:25]
        sc_b = cols[:, 25:26]
        un = ssb[:, 1536:1544]
        sg = ssb[:, 1544:1552]
        s2tok = None
        for c in range(NCH):
            pbk = pb[4 + (c % 2)]
            b.need(t, gtk, consts_tok, pbk.rd)
            ttk = b.inc(t.e.transpose(pbk.t[:, 0:8], gw[:, ts(c)], fident[0:8, 0:8]), "tm")
            pbk.wr = ttk
            gtkc = b.dma(g, sc_g, W["L0_ffn_ln_gamma"][0:1, ts(c)].rearrange(
                "a (p d) -> (a p) d", d=1), "dgC", allow_slow_non_contiguous=True)
            btkc = b.dma(g, sc_b, W["L0_ffn_ln_beta"][0:1, ts(c)].rearrange(
                "a (p d) -> (a p) d", d=1), "dgC", allow_slow_non_contiguous=True)
            b.need(v, ttk, gtkc, btkc, s2tok)
            cp1 = b.inc(v.e.tensor_copy(out=un, in_=pbk.t[:, 0:8]), "vm")
            b.need(v, cp1)
            cp2 = b.inc(v.e.tensor_scalar(out=sg, in0=un, scalar1=sc_g, scalar2=None,
                                          op0=ALU.mult), "vm")
            pbk.rd = cp2
            b.need(v, cp2)
            cp3 = b.inc(v.e.tensor_copy(out=gwp[:, c * 8:(c + 1) * 8], in_=sg), "vm")
            b.need(t, cp2, cp3)
            t.e.matmul(pb[0].t[0:1, 0:8], cpool[:, 3:4], sg,
                       start=(c == 0), stop=(c == NCH - 1))
            ins = t.e.matmul(pb[1].t[0:1, 0:8], sc_b, un,
                             start=(c == 0), stop=(c == NCH - 1))
            s2tok = b.inc(ins, "tm")
            b.need(g, s2tok)
        b.need(v, s2tok)
        v.e.tensor_copy(out=crow[0:1, 0:8], in_=pb[0].t[0:1, 0:8])
        rtk = b.inc(v.e.tensor_copy(out=crow[0:1, 8:16], in_=pb[1].t[0:1, 0:8]), "vm")
        pb[0].rd = rtk
        pb[1].rd = rtk
        b.need(t, rtk)
        t.e.matmul(pb[0].t[:, 0:8], sones[0:1, 0:128], crow[0:1, 0:8],
                   start=True, stop=True)
        btk = b.inc(t.e.matmul(pb[1].t[:, 0:8], sones[0:1, 0:128], crow[0:1, 8:16],
                               start=True, stop=True), "tm")
        pb[0].wr = btk
        pb[1].wr = btk
        b.need(v, btk)
        v.e.tensor_copy(out=s23[:, 0:8], in_=pb[0].t[:, 0:8])
        ctk = b.inc(v.e.tensor_copy(out=s23[:, 8:16], in_=pb[1].t[:, 0:8]), "vm")
        pb[0].rd = ctk
        pb[1].rd = ctk

    def load_x(seq):
        dst = bass.AP(resid, 0, [[NT * 1024, 128], [1024, NT], [1, 1024]])
        sap = bass.AP(x_in.tensor, seq * T * D, [[1024, 128], [128 * 1024, NT], [1, 1024]])
        waits = (b.tok("vm"), b.tok("sm"), b.tok("tm")) if seq > 0 else ()
        return b.dma(g, dst, sap, "dgx", waits=waits)

    def make_xfm(xtok, chunks=NCH):
        stk = None
        get = xtok if callable(xtok) else (lambda ti: xtok)
        for ti in range(NT):
            for half in range(chunks // 4):
                bank = pb[(2 * ti + half) % 4]
                b.need(t, get(ti), bank.rd, consts_tok)
                for j in range(4):
                    c = half * 4 + j
                    ins = t.e.matmul(bank.t[:, ts(j)],
                                     resid[:, ti * 1024 + c * 128: ti * 1024 + (c + 1) * 128],
                                     sident[:, :], start=True, stop=True)
                bank.wr = b.inc(ins, "tm")
                b.need(s, bank.wr)
                dst = bass.AP(xfm, half * 4 * T + ti * 128,
                              [[NCH * T, 128], [T, 4], [1, 128]])
                stk = b.inc(s.e.copy(out=dst, in_=bank.t[:, 0:512]), "sm")
                bank.rd = stk
        return stk

    def xsl(c, tg):
        return xfm[:, c * T + tg * 512:c * T + (tg + 1) * 512]

    def xslt(c, ti):
        return xfm[:, c * T + ti * 128:c * T + (ti + 1) * 128]

    ep_state = {"apply": None, "dy": None}

    def ln_epilogue(seq, ti, src0, src1, final, banks=()):
        sc = cols
        ssl = ssb[:, 0:1024]
        b.need(v, src0[1], src1[1], ep_state["apply"])
        a0 = b.inc(v.e.scalar_tensor_tensor(
            out=ssl[:, 0:512], in0=src0[0], scalar=1.0,
            in1=resid[:, ti * 1024: ti * 1024 + 512], op0=ALU.mult, op1=ALU.add,
            accum_out=sc[:, 0:1]), "vm")
        a1 = b.inc(v.e.scalar_tensor_tensor(
            out=ssl[:, 512:1024], in0=src1[0], scalar=1.0,
            in1=resid[:, ti * 1024 + 512: (ti + 1) * 1024], op0=ALU.mult, op1=ALU.add,
            accum_out=sc[:, 1:2]), "vm")
        for bk in banks:
            bk.rd = a1
        b.need(v, a0, a1)
        q0 = b.inc(v.e.scalar_tensor_tensor(
            out=pbu[:, 0:512], in0=ssl[:, 0:512], scalar=1.0, in1=ssl[:, 0:512],
            op0=ALU.mult, op1=ALU.mult, accum_out=sc[:, 2:3]), "vm")
        q1 = b.inc(v.e.scalar_tensor_tensor(
            out=pbu[:, 512:1024], in0=ssl[:, 512:1024], scalar=1.0, in1=ssl[:, 512:1024],
            op0=ALU.mult, op1=ALU.mult, accum_out=sc[:, 3:4]), "vm")
        b.need(v, a0, a1, q0, q1)
        v.e.tensor_add(out=sc[:, 4:5], in0=sc[:, 0:1], in1=sc[:, 1:2])
        m1 = b.inc(v.e.tensor_add(out=sc[:, 5:6], in0=sc[:, 2:3], in1=sc[:, 3:4]), "vm")
        b.need(v, m1)
        v.e.tensor_scalar(out=sc[:, 6:7], in0=sc[:, 4:5], scalar1=1.0 / 1024,
                          scalar2=None, op0=ALU.mult)
        m2 = b.inc(v.e.tensor_scalar(out=sc[:, 7:8], in0=sc[:, 5:6], scalar1=1.0 / 1024,
                                     scalar2=None, op0=ALU.mult), "vm")
        b.need(v, m2)
        m3 = b.inc(v.e.tensor_mul(out=sc[:, 8:9], in0=sc[:, 6:7], in1=sc[:, 6:7]), "vm")
        b.need(v, m3)
        m4 = b.inc(v.e.tensor_sub(out=sc[:, 9:10], in0=sc[:, 7:8], in1=sc[:, 8:9]), "vm")
        b.need(s, m4)
        m5 = b.inc(s.e.activation(out=sc[:, 10:11], in_=sc[:, 9:10], func=AF.Sqrt,
                                  bias=cpool[:, CP_EPS:CP_EPS + 1], scale=1.0), "sm")
        b.need(v, m5)
        m6 = b.inc(v.e.reciprocal(out=sc[:, 11:12], in_=sc[:, 10:11]), "vm")
        b.need(v, m6)
        ap_tok = b.inc(v.e.tensor_scalar(out=resid[:, ti * 1024:(ti + 1) * 1024],
                                         in0=ssl, scalar1=sc[:, 6:7], scalar2=sc[:, 11:12],
                                         op0=ALU.subtract, op1=ALU.mult), "vm")
        if final:
            b.need(v, ep_state["dy"])
            of_tok = b.inc(v.e.tensor_scalar(out=smal[:, 0:1024],
                                             in0=ssl, scalar1=sc[:, 6:7], scalar2=sc[:, 11:12],
                                             op0=ALU.subtract, op1=ALU.mult), "vm")
            dst = bass.AP(y_out.tensor, seq * T * D + ti * 128 * 1024,
                          [[1024, 128], [1, 1024]])
            ep_state["dy"] = b.dma(q, dst, smal[:, 0:1024], "dy", waits=(of_tok,))
            ap_tok = of_tok
        ep_state["apply"] = ap_tok
        ep_state.setdefault("tiles", {})[ti] = ap_tok
        return ap_tok

    def block_l0a(seq, final):
        b.dma_barrier(q, g, light=True)
        tls = ep_state.get("tiles", {})
        fallback = b.tok("vm")
        xftok = make_xfm(lambda ti: tls.get(ti, fallback))
        st = {"qkv_tok": None}

        def diag_m(h):
            return smask[:, :]

        def out_fm_l(h, qg):
            hp = h // 2
            return bigA[:, hp * 2048 + qg * 512:hp * 2048 + (qg + 1) * 512]

        def load_lqkv(g2):
            base = (g2 % 2) * 1536
            toks = []
            for j, col0 in enumerate((g2 * 128, 512 + g2 * 128, 1024 + g2 * 128)):
                dst = bass.AP(wbuf, base + j * 512, [[16384, 128], [128, 4], [1, 128]])
                srcw = bass.AP(W["L0_attn_w_lqkv"].tensor, col0,
                               [[1536, 128], [1536 * 128, 4], [1, 128]])
                toks.append(b.dma(g, dst, srcw, "gl", waits=(b.tok("tm"),) if j == 0 else ()))
            rbs = 1280 + (g2 % 2) * 384
            btk = None
            for j, col0 in enumerate((g2 * 128, 512 + g2 * 128, 1024 + g2 * 128)):
                btk = b.dma(q, smal[0:1, rbs + j * 128:rbs + (j + 1) * 128],
                            W["L0_attn_b_lqkv"][0:1, col0:col0 + 128], "qr",
                            waits=(b.tok("tm"), b.tok("qr")) if j == 0 else ())
            return toks[-1], btk, rbs

        def lqkv_fillers(g2, wtok, btk, rbs):
            base = (g2 % 2) * 1536
            bufb = (g2 % 2) * 6144
            fill = []
            for j, scale in ((0, True), (1, False)):
                for tg in range(4):
                    def go(j=j, tg=tg, scale=scale):
                        bank = pb[6]
                        b.need(t, wtok, btk, xftok, bank.rd)
                        for c in range(4):
                            t.e.matmul(bank.t[:, :],
                                       wbuf[:, base + j * 512 + c * 128:
                                            base + j * 512 + (c + 1) * 128],
                                       xsl(c, tg), start=(c == 0), stop=False)
                        ins = t.e.matmul(bank.t[:, :],
                                         smal[0:1, rbs + j * 128:rbs + (j + 1) * 128],
                                         sones[0:1, 0:512], start=False, stop=True)
                        mtk = b.inc(ins, "tm")
                        bank.wr = mtk
                        b.need(v, mtk)
                        dstc = bigB[:, bufb + j * 2048 + tg * 512:
                                    bufb + j * 2048 + (tg + 1) * 512]
                        if scale:
                            etk = b.inc(v.e.tensor_scalar(out=dstc, in0=bank.t[:, :],
                                                          scalar1=0.125, scalar2=None,
                                                          op0=ALU.mult), "vm")
                        else:
                            etk = b.inc(v.e.tensor_copy(out=dstc, in_=bank.t[:, :]), "vm")
                        bank.rd = etk
                    fill.append(go)
            for ti in range(NT):
                def gov(ti=ti):
                    bank = pb[6]
                    b.need(t, wtok, btk, xftok, bank.rd)
                    for c in range(4):
                        t.e.matmul(bank.t[:, 0:128], xslt(c, ti),
                                   wbuf[:, base + 1024 + c * 128:base + 1024 + (c + 1) * 128],
                                   start=(c == 0), stop=False)
                    ins = t.e.matmul(bank.t[:, 0:128], sones[0:1, 0:128],
                                     smal[0:1, rbs + 256:rbs + 384], start=False, stop=True)
                    mtk = b.inc(ins, "tm")
                    bank.wr = mtk
                    b.need(v, mtk)
                    etk = b.inc(v.e.tensor_copy(
                        out=bigB[:, bufb + 4096 + ti * 128:bufb + 4096 + (ti + 1) * 128],
                        in_=bank.t[:, 0:128]), "vm")
                    bank.rd = etk
                fill.append(gov)
            return fill

        w0l, b0l, rb0l = load_lqkv(0)
        for go in lqkv_fillers(0, w0l, b0l, rb0l):
            go()
        for g2 in range(4):
            st["qkv_tok"] = b.tok("vm")
            if g2 < 3:
                wn, bn, rbn = load_lqkv(g2 + 1)
                fillers = lqkv_fillers(g2 + 1, wn, bn, rbn)
            else:
                fillers = []
            bufb = (g2 % 2) * 6144
            heads = [(g2 * 2 + j, 64 * j, 0, j) for j in range(2)]
            attn_softmax_pv(seq, heads, T, bigB, bufb, bufb + 2048, bigB,
                            bufb + 4096, 128,
                            [(bigC, 0, 8192), (bigC, 4096, 8192)],
                            out_fm_l, diag_m, None, None, st,
                            (wbuf, 6144), wbuf[:, 8192:12288].bitcast(F32),
                            wbuf[:, 14336:16384].bitcast(F32),
                            wbuf[:, 12288:14336].bitcast(F32), filler=fillers)
        lat_tok = b.tok("vm")

        def lproj_fillers():
            wl_rd = b.tok("tm")
            dstw = bass.AP(wbuf, 0, [[16384, 128], [512, 4], [1, 512]])
            srcw = bass.AP(W["L0_attn_w_lproj"].tensor, 0,
                           [[512, 128], [512 * 128, 4], [1, 512]])
            wltk = b.dma(g, dstw, srcw, "gp", waits=(wl_rd, b.tok("vm"), b.tok("sm")))
            bltk = b.dma(q, smal[0:1, 1280:1792], W["L0_attn_b_lproj"][0:1, :], "qr",
                         waits=(b.tok("tm"), b.tok("qr")))
            fill = []
            for oc in range(4):
                for tg in range(4):
                    def go(oc=oc, tg=tg):
                        bank = pb[6]
                        b.need(t, wltk, bltk, lat_tok, bank.rd)
                        for c in range(4):
                            t.e.matmul(bank.t[:, :],
                                       wbuf[:, c * 512 + oc * 128:c * 512 + (oc + 1) * 128],
                                       bigA[:, c * 2048 + tg * 512:c * 2048 + (tg + 1) * 512],
                                       start=(c == 0), stop=False)
                        ins = t.e.matmul(bank.t[:, :],
                                         smal[0:1, 1280 + oc * 128:1280 + (oc + 1) * 128],
                                         sones[0:1, 0:512], start=False, stop=True)
                        mtk = b.inc(ins, "tm")
                        bank.wr = mtk
                        b.need(v, mtk)
                        etk = b.inc(v.e.tensor_copy(
                            out=bigB[:, oc * 2048 + tg * 512:oc * 2048 + (tg + 1) * 512],
                            in_=bank.t[:, :]), "vm")
                        bank.rd = etk
                    fill.append(go)
            return fill

        wc_rd = b.tok("tm")
        ctoks = []
        for r in range(4):
            dst = bass.AP(wbuf, r * 4096, [[16384, 128], [512, 8], [1, 512]])
            src = bass.AP(W["L0_attn_w_comp"].tensor, r * 1024 * 1024 + 512,
                          [[1024, 128], [1024 * 128, 8], [1, 512]])
            ctoks.append(b.dma(g, dst, src, "gc", waits=(wc_rd, b.tok("vm"), b.tok("sm")) if r == 0 else ()))
        bctk = b.dma(q, smal[0:1, 1280:1792], W["L0_attn_b_comp"][0:1, 512:1024],
                     "qr", waits=(b.tok("tm"), b.tok("qr")))
        for oc in range(4):
            bank = pb[4]
            b.need(t, ctoks[-1], bctk, xftok, bank.rd)
            for r in range(4):
                for c in range(NCH):
                    t.e.matmul(bank.t[:, :],
                               wbuf[:, r * 4096 + c * 512 + oc * 128:
                                    r * 4096 + c * 512 + (oc + 1) * 128],
                               bass.AP(xfm, c * T + r, [[NCH * T, 128], [4, 512]]),
                               start=(r == 0 and c == 0), stop=False)
            ins = t.e.matmul(bank.t[:, :],
                             smal[0:1, 1280 + oc * 128:1280 + (oc + 1) * 128],
                             sones[0:1, 0:512], start=False, stop=True)
            mtk = b.inc(ins, "tm")
            bank.wr = mtk
            b.need(v, mtk)
            etk = b.inc(v.e.tensor_copy(out=bigC[:, oc * 512:(oc + 1) * 512],
                                        in_=bank.t[:, :]), "vm")
            bank.rd = etk
        cg_tok = b.tok("vm")

        wg_rd = b.tok("tm")
        dst = bass.AP(wbuf, 0, [[16384, 128], [1536, 4], [1, 1536]])
        src = bass.AP(W["L0_attn_w_gqkv"].tensor, 0, [[1536, 128], [1536 * 128, 4], [1, 1536]])
        wgtk = b.dma(g, dst, src, "gg", waits=(wg_rd, b.tok("vm"), b.tok("sm")))
        bgtk = b.dma(q, smal[0:1, 0:1536], W["L0_attn_b_gqkv"][0:1, :], "qr",
                     waits=(b.tok("tm"), b.tok("vm"), b.tok("qr")))
        for j, (dst_col, scale) in enumerate(((2048, True), (4096, False))):
            for d2 in range(4):
                bank = pb[(d2 % 2)]
                b.need(t, wgtk, bgtk, cg_tok, bank.rd)
                for c in range(4):
                    t.e.matmul(bank.t[:, :],
                               wbuf[:, c * 1536 + j * 512 + d2 * 128:
                                    c * 1536 + j * 512 + (d2 + 1) * 128],
                               bigC[:, c * 512:(c + 1) * 512],
                               start=(c == 0), stop=False)
                ins = t.e.matmul(bank.t[:, :],
                                 smal[0:1, j * 512 + d2 * 128:j * 512 + (d2 + 1) * 128],
                                 sones[0:1, 0:512], start=False, stop=True)
                mtk = b.inc(ins, "tm")
                bank.wr = mtk
                b.need(v, mtk)
                dstc = bigC[:, dst_col + d2 * 512:dst_col + (d2 + 1) * 512]
                if scale:
                    etk = b.inc(v.e.tensor_scalar(out=dstc, in0=bank.t[:, :],
                                                  scalar1=0.125, scalar2=None,
                                                  op0=ALU.mult), "vm")
                else:
                    etk = b.inc(v.e.tensor_copy(out=dstc, in_=bank.t[:, :]), "vm")
                bank.rd = etk
        for tct in range(4):
            bank = pb[2 + (tct % 2)]
            b.need(t, wgtk, bgtk, cg_tok, bank.rd)
            for c in range(4):
                t.e.matmul(bank.t[:, :],
                           bigC[:, c * 512 + tct * 128:c * 512 + (tct + 1) * 128],
                           wbuf[:, c * 1536 + 1024:c * 1536 + 1536],
                           start=(c == 0), stop=False)
            ins = t.e.matmul(bank.t[:, :], sones[0:1, 0:128],
                             smal[0:1, 1024:1536], start=False, stop=True)
            mtk = b.inc(ins, "tm")
            bank.wr = mtk
            b.need(v, mtk)
            etk = b.inc(v.e.tensor_copy(out=bigC[:, 6144 + tct * 512:6144 + (tct + 1) * 512],
                                        in_=bank.t[:, :]), "vm")
            bank.rd = etk
        st["qkv_tok"] = b.tok("vm")

        def out_fm_g(h, qg):
            hp = h // 2
            return bigB[:, 12288 + hp * 512:12288 + (hp + 1) * 512]

        heads_g = [(h, 64 * (h % 2), h // 2, h) for h in range(8)]
        lp_fill = lproj_fillers()
        attn_softmax_pv(seq, heads_g, 512, bigC, 2048, 4096, bigC, 6144, 512,
                        [(wbuf, 8192, 16384), (bigB, 8192, 16384)],
                        out_fm_g, diag_m, None, None, st,
                        (wbuf, 10240), wbuf[:, 12288:13312].bitcast(F32),
                        wbuf[:, 15360:16384].bitcast(F32),
                        wbuf[:, 13312:15360].bitcast(F32), filler=lp_fill)
        gat_tok = b.tok("vm")

        wp_rd = b.tok("tm")
        dst = bass.AP(wbuf, 0, [[16384, 128], [512, 4], [1, 512]])
        src = bass.AP(W["L0_attn_w_gproj"].tensor, 0, [[512, 128], [512 * 128, 4], [1, 512]])
        wptk = b.dma(g, dst, src, "gp", waits=(wp_rd, b.tok("vm"), b.tok("sm")))
        bptk = b.dma(q, smal[0:1, 1280:1792], W["L0_attn_b_gproj"][0:1, :], "qr",
                     waits=(b.tok("tm"), b.tok("qr")))
        for oc in range(4):
            bank = pb[(oc % 2)]
            b.need(t, wptk, bptk, gat_tok, bank.rd)
            for c in range(4):
                t.e.matmul(bank.t[:, :],
                           wbuf[:, c * 512 + oc * 128:c * 512 + (oc + 1) * 128],
                           bigB[:, 12288 + c * 512:12288 + (c + 1) * 512],
                           start=(c == 0), stop=False)
            ins = t.e.matmul(bank.t[:, :],
                             smal[0:1, 1280 + oc * 128:1280 + (oc + 1) * 128],
                             sones[0:1, 0:512], start=False, stop=True)
            mtk = b.inc(ins, "tm")
            bank.wr = mtk
            b.need(v, mtk)
            etk = b.inc(v.e.tensor_copy(out=bigB[:, 14336 + oc * 512:14336 + (oc + 1) * 512],
                                        in_=bank.t[:, :]), "vm")
            bank.rd = etk
        b.need(v, b.tok("vm"))
        rep_tok = None
        for ch in range(4):
            for r in range(4):
                dstr = bass.AP(bigA, 8192 + ch * 2048 + r, [[16384, 128], [4, 512]])
                rep_tok = b.inc(v.e.tensor_copy(
                    out=dstr, in_=bigB[:, 14336 + ch * 512:14336 + (ch + 1) * 512]), "vm")

        wgt_rd = b.tok("tm")
        dstg = bass.AP(wbuf, 0, [[16384, 128], [2, 8], [1, 2]])
        srcg = bass.AP(W["L0_attn_w_gate"].tensor, 0, [[2, 128], [2 * 128, 8], [1, 2]])
        wgttk = b.dma(g, dstg, srcg, "gt", waits=(wgt_rd, b.tok("vm"), b.tok("sm")))
        bgttk = b.dma(q, smal[0:1, 1900:1902], W["L0_attn_b_gate"][0:1, :], "qr",
                      waits=(b.tok("tm"), b.tok("qr")))
        for tgg in range(4):
            for e in range(2):
                bank = pb[4 + e]
                b.need(t, wgttk, bgttk, rep_tok, bank.rd)
                for c in range(8):
                    srcc = (bigB[:, c * 2048 + tgg * 512:c * 2048 + (tgg + 1) * 512]
                            if c < 4 else
                            bigA[:, 8192 + (c - 4) * 2048 + tgg * 512:
                                 8192 + (c - 4) * 2048 + (tgg + 1) * 512])
                    t.e.matmul(bank.t[0:1, :], wbuf[:, c * 2 + e:c * 2 + e + 1],
                               srcc, start=(c == 0), stop=False)
                ins = t.e.matmul(bank.t[0:1, :], smal[0:1, 1900 + e:1901 + e],
                                 sones[0:1, 0:512], start=False, stop=True)
                bank.wr = b.inc(ins, "tm")
            b.need(v, pb[4].wr, pb[5].wr)
            c1 = b.inc(v.e.tensor_copy(out=ssb[0:1, 1024:1536], in_=pb[5].t[0:1, :]), "vm")
            pb[5].rd = c1
            b.need(v, c1)
            c2 = b.inc(v.e.scalar_tensor_tensor(out=ssb[0:1, 1536:2048],
                                                in0=pb[4].t[0:1, :], scalar=1.0,
                                                in1=ssb[0:1, 1024:1536],
                                                op0=ALU.mult, op1=ALU.subtract), "vm")
            pb[4].rd = c2
            b.need(s, c2)
            c3 = b.inc(s.e.activation(out=smal[0:1, 1024:1536],
                                      in_=ssb[0:1, 1536:2048], func=AF.Sigmoid), "sm")
            b.need(v, c3)
            c4 = b.inc(v.e.tensor_scalar(out=smal[0:1, 1536:2048],
                                         in0=smal[0:1, 1024:1536],
                                         scalar1=-1.0, scalar2=1.0,
                                         op0=ALU.mult, op1=ALU.add), "vm")
            for half, (buf, base, gcol) in enumerate(((bigB, 0, 1024), (bigA, 8192, 1536))):
                for ch in range(4):
                    bank = pb[(ch % 2)]
                    b.need(t, c3 if half == 0 else c4, bank.rd)
                    mtk = b.inc(t.e.matmul(bank.t[:, :], sones[0:1, 0:128],
                                           smal[0:1, gcol:gcol + 512],
                                           start=True, stop=True), "tm")
                    bank.wr = mtk
                    b.need(v, mtk, b.tok("vm"))
                    sl = buf[:, base + ch * 2048 + tgg * 512:base + ch * 2048 + (tgg + 1) * 512]
                    etk = b.inc(v.e.tensor_mul(out=sl, in0=sl, in1=bank.t[:, :]), "vm")
                    bank.rd = etk
        bo_tok = b.tok("vm")

        for ti in range(NT):
            for half, bank in ((0, pb[6]), (1, pb[7])):
                b.need(t, bo_tok, bank.rd, consts_tok)
                for j in range(4):
                    c = half * 4 + j
                    srcc = (bigB[:, c * 2048 + ti * 128:c * 2048 + (ti + 1) * 128]
                            if c < 4 else
                            bigA[:, 8192 + (c - 4) * 2048 + ti * 128:
                                 8192 + (c - 4) * 2048 + (ti + 1) * 128])
                    ins = t.e.matmul(bank.t[:, ts(j)], srcc, sident[:, :],
                                     start=True, stop=True)
                bank.wr = b.inc(ins, "tm")
            ln_epilogue(seq, ti, (pb[6].t[:, :], pb[6].wr),
                        (pb[7].t[:, :], pb[7].wr), final, banks=(pb[6], pb[7]))

    RT = 1160

    def router_tile(seq, ti, xftok):
        sc = cols
        rs = resid[:, ti * 1024:(ti + 1) * 1024]
        b.need(v, ep_state["apply"])
        t0 = b.inc(v.e.tensor_reduce(out=sc[:, 0:1], in_=rs, axis=AX.X, op=ALU.add), "vm")
        t1 = b.inc(v.e.scalar_tensor_tensor(out=pbu[:, 0:1024], in0=rs, scalar=1.0,
                                            in1=rs, op0=ALU.mult, op1=ALU.mult,
                                            accum_out=sc[:, 1:2]), "vm")
        b.need(v, t0, t1)
        v.e.tensor_scalar(out=sc[:, 2:3], in0=sc[:, 0:1], scalar1=1.0 / 1024,
                          scalar2=None, op0=ALU.mult)
        t2 = b.inc(v.e.tensor_scalar(out=sc[:, 3:4], in0=sc[:, 1:2], scalar1=1.0 / 1024,
                                     scalar2=None, op0=ALU.mult), "vm")
        b.need(v, t2)
        t3 = b.inc(v.e.tensor_mul(out=sc[:, 4:5], in0=sc[:, 2:3], in1=sc[:, 2:3]), "vm")
        b.need(v, t3)
        t4 = b.inc(v.e.tensor_sub(out=sc[:, 5:6], in0=sc[:, 3:4], in1=sc[:, 4:5]), "vm")
        b.need(s, t4)
        t5 = b.inc(s.e.activation(out=sc[:, 6:7], in_=sc[:, 5:6], func=AF.Sqrt,
                                  bias=cpool[:, 0:1], scale=1.0), "sm")
        b.need(v, t5)
        t6 = b.inc(v.e.reciprocal(out=sc[:, 7:8], in_=sc[:, 6:7]), "vm")
        b.need(v, t6)
        t7 = b.inc(v.e.scalar_tensor_tensor(out=sc[:, 8:9], in0=sc[:, 2:3], scalar=-1.0,
                                            in1=sc[:, 7:8], op0=ALU.mult, op1=ALU.mult), "vm")
        bank = pb[4 + (ti % 2)]
        b.need(t, xftok, bank.rd)
        for c in range(NCH):
            ins = t.e.matmul(bank.t[:, 0:8], xslt(c, ti), gwp[:, c * 8:(c + 1) * 8],
                             start=(c == 0), stop=(c == NCH - 1))
        mtk = b.inc(ins, "tm")
        bank.wr = mtk
        lgA = smal[:, RT:RT + 8]
        lgB = smal[:, RT + 8:RT + 16]
        lg = smal[:, RT + 16:RT + 24]
        pr = smal[:, RT + 24:RT + 32]
        gg = smal[:, RT + 32:RT + 40]
        mk = smal[:, RT + 40:RT + 48]
        b.need(v, mtk, t6, t7)
        u0 = b.inc(v.e.tensor_scalar(out=lgA, in0=bank.t[:, 0:8], scalar1=sc[:, 7:8],
                                     scalar2=None, op0=ALU.mult), "vm")
        bank.rd = u0
        b.need(v, u0)
        u1 = b.inc(v.e.scalar_tensor_tensor(out=lgB, in0=s23[:, 0:8], scalar=sc[:, 8:9],
                                            in1=lgA, op0=ALU.mult, op1=ALU.add), "vm")
        b.need(v, u1)
        u2 = b.inc(v.e.tensor_add(out=lg, in0=lgB, in1=s23[:, 8:16]), "vm")
        b.need(v, u2)
        u3 = b.inc(v.e.tensor_reduce(out=sc[:, 9:10], in_=lg, axis=AX.X, op=ALU.max,
                                     negate=True), "vm")
        b.need(s, u2, u3)
        u4 = b.inc(s.e.activation(out=pr, in_=lg, func=AF.Exp, bias=sc[:, 9:10],
                                  scale=1.0, accum_out=sc[:, 10:11]), "sm")
        b.need(v, u4)
        u5 = b.inc(v.e.reciprocal(out=sc[:, 11:12], in_=sc[:, 10:11]), "vm")
        b.need(v, u5)
        u6 = b.inc(v.e.tensor_scalar(out=pr, in0=pr, scalar1=sc[:, 11:12],
                                     scalar2=None, op0=ALU.mult), "vm")
        b.need(v, u6)
        u7 = b.inc(v.e.tensor_reduce(out=sc[:, 12:13], in_=pr, axis=AX.X, op=ALU.max), "vm")
        b.need(v, u7)
        u8 = b.inc(v.e.tensor_scalar(out=gg, in0=pr, scalar1=sc[:, 12:13],
                                     scalar2=None, op0=ALU.is_ge), "vm")
        b.need(v, u8)
        u9 = b.inc(v.e.scalar_tensor_tensor(out=mk, in0=gg, scalar=NEG, in1=pr,
                                            op0=ALU.mult, op1=ALU.add), "vm")
        b.need(v, u9)
        uA = b.inc(v.e.tensor_reduce(out=sc[:, 13:14], in_=mk, axis=AX.X, op=ALU.max), "vm")
        b.need(v, uA)
        uB = b.inc(v.e.tensor_scalar(out=gg, in0=pr, scalar1=sc[:, 13:14],
                                     scalar2=None, op0=ALU.is_ge), "vm")
        b.need(v, uB)
        uC = b.inc(v.e.tensor_mul(out=smal[:, 1024 + ti * 8:1024 + (ti + 1) * 8],
                                  in0=gg, in1=pr), "vm")
        return uC

    def block_l0f(seq, final):
        b.dma_barrier(q, g, light=True)
        tls = ep_state.get("tiles", {})
        fallback = b.tok("vm")
        xftok = make_xfm(lambda ti: tls.get(ti, fallback))
        ro_done = [None] * 4
        def ld_w1(e, waits):
            tk = None
            for f4 in range(4):
                tk = b.dma(q, bass.AP(wbuf, f4 * 4096, [[16384, 128], [1024, 4], [1, 1024]]),
                           bass.AP(scr["m1"].tensor, (e * 2048 + f4 * 512) * 1024,
                                   [[1024, 128], [128 * 1024, 4], [1, 1024]]),
                           "qa", waits=waits if f4 == 0 else ())
            return tk

        def ld_w2(e, waits):
            tk = None
            for f4 in range(4):
                tk = b.dma(q, bass.AP(bigB, f4 * 4096, [[16384, 128], [1024, 4], [1, 1024]]),
                           bass.AP(scr["m2"].tensor, (e * 2048 + f4 * 512) * 1024,
                                   [[1024, 128], [128 * 1024, 4], [1, 1024]]),
                           "qb", waits=waits if f4 == 0 else ())
            return tk

        w1tks = [None] * 8
        w2tks = [None] * 8
        w1tks[0] = ld_w1(0, (b.tok("tm"),))
        w2tks[0] = ld_w2(0, (b.tok("tm"),))
        for e in range(8):
            w1tk = w1tks[e]
            w2tk = w2tks[e]
            for tg in range(4):
                if e == 0:
                    for ti in range(tg * 4, tg * 4 + 4):
                        ro_done[tg] = router_tile(seq, ti, xftok)
                for f in range(16):
                    bank = pb[f % 2]
                    b.need(t, w1tk, xftok, bank.rd)
                    for c in range(NCH):
                        ins = t.e.matmul(bank.t[:, :], wbuf[:, f * 1024 + c * 128:f * 1024 + (c + 1) * 128],
                                         xsl(c, tg), start=(c == 0), stop=(c == NCH - 1))
                    mtk = b.inc(ins, "tm")
                    bank.wr = mtk
                    b.need(s, mtk)
                    etk = b.inc(s.e.activation(out=bigC[:, f * 512:(f + 1) * 512],
                                               in_=bank.t[:, :], func=AF.Gelu), "sm")
                    bank.rd = etk
                hdone = b.tok("sm")
                if tg == 3 and e < 7:
                    w1tks[e + 1] = ld_w1(e + 1, (b.tok("tm"),))
                for tt in range(4):
                    ti = tg * 4 + tt
                    comb_col = smal[:, 1024 + ti * 8 + e:1024 + ti * 8 + e + 1]
                    for ch in range(2):
                        bank = pb[2 + ch]
                        b.need(t, w2tk, bank.rd, hdone)
                        for f in range(16):
                            ins = t.e.matmul(bank.t[:, :],
                                             bigC[:, f * 512 + tt * 128:f * 512 + (tt + 1) * 128],
                                             bigB[:, f * 1024 + ch * 512:f * 1024 + (ch + 1) * 512],
                                             start=(f == 0), stop=(f == 15))
                        mtk = b.inc(ins, "tm")
                        bank.wr = mtk
                        dstc = bigA[:, ti * 1024 + ch * 512:ti * 1024 + (ch + 1) * 512]
                        b.need(v, mtk)
                        if e == 0:
                            b.need(v, ro_done[tg])
                            etk = b.inc(v.e.tensor_scalar(out=dstc, in0=bank.t[:, :],
                                                          scalar1=comb_col, scalar2=None,
                                                          op0=ALU.mult), "vm")
                        else:
                            b.need(v, b.tok("vm"))
                            etk = b.inc(v.e.scalar_tensor_tensor(out=dstc, in0=bank.t[:, :],
                                                                 scalar=comb_col, in1=dstc,
                                                                 op0=ALU.mult, op1=ALU.add), "vm")
                        bank.rd = etk
            if e < 7:
                w2tks[e + 1] = ld_w2(e + 1, (b.tok("tm"),))
        acc_tok = b.tok("vm")
        for ti in range(NT):
            ln_epilogue(seq, ti, (bigA[:, ti * 1024:ti * 1024 + 512], acc_tok),
                        (bigA[:, ti * 1024 + 512:(ti + 1) * 1024], acc_tok), final)

    def attn_softmax_pv(seq, heads, Tq, qk_buf, qbuf_col, kbuf_col, v_buf,
                        vbuf_col, vstride, pts, out_fm, diag_bias, low_bias,
                        far_col, state, palt, salt, salt2, rsc, filler=None):
        nqt = Tq // 128

        def qh(rbase, d2, qt):
            return qk_buf[rbase:rbase + 64,
                          qbuf_col + d2 * Tq + qt * 128: qbuf_col + d2 * Tq + (qt + 1) * 128]

        def kh(rbase, d2, c0, w):
            return qk_buf[rbase:rbase + 64, kbuf_col + d2 * Tq + c0: kbuf_col + d2 * Tq + c0 + w]

        def prow(r, c0, c1):
            if r == 0:
                return pbu[:, c0:c1]
            if r == 1:
                return palt[0][:, palt[1] + c0:palt[1] + c1]
            return pbu[:, 1024 + c0:1024 + c1]

        def srow(r, c0, c1):
            return (ssb, salt, salt2)[r][:, c0:c1]

        def do_chain(job):
            hs, qsub, qg, r, lix = job["hs"], job["qsub"], job["qg"], job["r"], job["lix"]
            h, rbase, d2, hh = job["head"]
            qt = 4 * qg + qsub
            klen = (qt + 1) * 128
            sc_nm = cols[:, r:r + 1]
            sc_l = cols[:, 32 + lix:33 + lix]
            b.need(v, state.get("srd" + str(r)))
            for kg in range(0, qt + 1, 4):
                w = min(512, klen - kg * 128)
                bank = pb[(kg // 4) % 2]
                b.need(t, state["qkv_tok"], bank.rd)
                mtk = b.inc(t.e.matmul(bank.t[:, 0:w], qh(rbase, d2, qt),
                                       kh(rbase, d2, kg * 128, w),
                                       start=True, stop=True), "tm")
                bank.wr = mtk
                b.need(v, mtk, state.get("brel_tok"), state.get("brel_low"))
                nt_in = w // 128
                nfar = 0
                while kg + nfar < qt - 1 and nfar < nt_in:
                    nfar += 1
                if nfar > 0:
                    if far_col is not None:
                        etk = v.e.tensor_scalar(
                            out=srow(r, kg * 128, (kg + nfar) * 128),
                            in0=bank.t[:, 0:nfar * 128], scalar1=far_col(h),
                            scalar2=None, op0=ALU.add)
                    else:
                        etk = v.e.tensor_copy(
                            out=srow(r, kg * 128, (kg + nfar) * 128),
                            in_=bank.t[:, 0:nfar * 128])
                for j in range(nfar, nt_in):
                    kt = kg + j
                    if kt == qt - 1 and low_bias is not None:
                        etk = v.e.tensor_tensor(
                            out=srow(r, kt * 128, (kt + 1) * 128),
                            in0=bank.t[:, j * 128:(j + 1) * 128],
                            in1=low_bias(h), op=ALU.add)
                    elif kt == qt:
                        etk = v.e.tensor_tensor(
                            out=srow(r, kt * 128, (kt + 1) * 128),
                            in0=bank.t[:, j * 128:(j + 1) * 128],
                            in1=diag_bias(h), op=ALU.add)
                    elif kt == qt - 1:
                        etk = v.e.tensor_copy(
                            out=srow(r, kt * 128, (kt + 1) * 128),
                            in_=bank.t[:, j * 128:(j + 1) * 128])
                bank.rd = b.inc(etk, "vm")
            b.need(v, b.tok("vm"), state.get("prd" + str(r)))
            mx = b.inc(v.e.tensor_scalar(out=prow(r, 0, klen),
                                         in0=srow(r, 0, klen), scalar1=-1.0,
                                         scalar2=None, op0=ALU.mult,
                                         op1=ALU.min, accum_out=sc_nm), "vm")
            b.need(s, mx)
            ex = b.inc(s.e.activation(out=prow(r, 0, klen), in_=srow(r, 0, klen),
                                      func=AF.Exp, bias=sc_nm, scale=1.0,
                                      accum_out=sc_l), "sm")
            state["srd" + str(r)] = ex
            job["ex"] = ex

        def do_transposes(job):
            hs, qsub, qg, r = job["hs"], job["qsub"], job["qg"], job["r"]
            qt = 4 * qg + qsub
            ptt, ptc, ptp = job["pt"]
            for kt0 in range(0, qt + 1, 4):
                nb = min(4, qt + 1 - kt0)
                bank = pb[4 + (kt0 // 4) % 2]
                b.need(t, job["ex"], bank.rd, consts_tok)
                for j in range(nb):
                    ins = t.e.matmul(bank.t[:, ts(j)],
                                     prow(r, (kt0 + j) * 128, (kt0 + j + 1) * 128),
                                     sident[:, :], start=True, stop=True)
                bank.wr = b.inc(ins, "tm")
                eng2 = s if (kt0 // 4) % 2 == 0 else v
                b.need(eng2, bank.wr)
                dst = bass.AP(ptt, ptc + kt0 * 512 + qsub * 128,
                              [[ptp, 128], [512, nb], [1, 128]])
                if eng2 is s:
                    ctk = b.inc(s.e.copy(out=dst, in_=bank.t[:, 0:nb * 128]), "sm")
                else:
                    ctk = b.inc(v.e.tensor_copy(out=dst, in_=bank.t[:, 0:nb * 128]), "vm")
                bank.rd = ctk
            state["prd" + str(r)] = b.tok("tm")

        def do_pv(head, hi, qg, pt, rbase_cols):
            h, rbase, d2, hh = head
            ktmax = 4 * qg + 3
            ptt, ptc, ptp = pt
            rtok = b.tok("vm")
            bank7 = pb[7]
            b.need(t, rtok, bank7.rd, consts_tok)
            for qsub in range(4):
                ins = t.e.matmul(bank7.t[0:1, qsub * 128:(qsub + 1) * 128],
                                 cols[:, rbase_cols + qsub:rbase_cols + qsub + 1],
                                 fident[:, :], start=True, stop=True)
            rt1 = b.inc(ins, "tm")
            state["rcols_rd"] = rt1
            bank7.wr = rt1
            b.need(s, rt1)
            rs1 = b.inc(s.e.copy(out=rsc[0:1, 0:512], in_=bank7.t[0:1, 0:512]), "sm")
            bank7.rd = rs1
            b.need(t, rs1)
            rt2 = b.inc(t.e.matmul(bank7.t[:, :], sones[0:1, 0:128],
                                   rsc[0:1, 0:512], start=True, stop=True), "tm")
            bank7.wr = rt2
            b.need(s, rt2, state.get("rsc_rd"))
            rs2 = b.inc(s.e.copy(out=rsc[:, 512:1024], in_=bank7.t[:, :]), "sm")
            bank7.rd = rs2
            bank = pb[2 + (hi % 2)]
            b.need(t, b.tok("sm"), b.tok("vm"), bank.rd)
            for kt in range(ktmax + 1):
                off = max(0, kt - 4 * qg) * 128
                ins = t.e.matmul(bank.t[rbase:rbase + 64, off:512],
                                 v_buf[:, vbuf_col + kt * vstride + hh * 64:
                                       vbuf_col + kt * vstride + hh * 64 + 64],
                                 ptt[:, ptc + kt * 512 + off:ptc + (kt + 1) * 512],
                                 start=(kt == 0), stop=(kt == ktmax))
            mtk = b.inc(ins, "tm")
            bank.wr = mtk
            b.need(v, mtk, rs2)
            etk = b.inc(v.e.tensor_mul(out=out_fm(h, qg)[rbase:rbase + 64, :],
                                       in0=rsc[rbase:rbase + 64, 512:1024],
                                       in1=bank.t[rbase:rbase + 64, :]), "vm")
            bank.rd = etk
            state["rsc_rd"] = etk

        rown = [0]

        def next_row(maxrows):
            r = rown[0] % maxrows
            rown[0] += 1
            return r

        for p0 in range(0, len(heads), 2):
            pair = heads[p0:p0 + 2]
            for qg in range(nqt // 4):
                klen_max = (4 * qg + 4) * 128
                paired = PAIR_ATTN and len(pair) == 2 and klen_max <= 1024
                if paired:
                    jobs = []
                    for qsub in range(4):
                        for hs, head in enumerate(pair):
                            jobs.append({"hs": hs, "head": head, "qsub": qsub,
                                         "qg": qg, "r": next_row(3),
                                         "lix": qsub * 2 + hs, "pt": pts[hs]})
                    pend = []
                    for job in jobs:
                        do_chain(job)
                        if filler:
                            filler.pop(0)()
                        if len(pend) >= 2:
                            do_transposes(pend.pop(0))
                        pend.append(job)
                    for job in pend:
                        do_transposes(job)
                    b.need(v, jobs[-1]["ex"], state.get("rcols_rd"))
                    for job in jobs:
                        b.inc(v.e.reciprocal(
                            out=cols[:, 16 + job["hs"] * 4 + job["qsub"]:
                                     17 + job["hs"] * 4 + job["qsub"]],
                            in_=cols[:, 32 + job["lix"]:33 + job["lix"]]), "vm")
                    for hs, head in enumerate(pair):
                        do_pv(head, p0 + hs, qg, pts[hs], 16 + hs * 4)
                else:
                    for hs, head in enumerate(pair):
                        maxr = 3 if klen_max <= 1024 else 2
                        jobs = []
                        for qsub in range(4):
                            jobs.append({"hs": hs, "head": head, "qsub": qsub,
                                         "qg": qg, "r": next_row(maxr),
                                         "lix": qsub, "pt": pts[0]})
                        pend = []
                        for job in jobs:
                            do_chain(job)
                            if filler:
                                filler.pop(0)()
                            if len(pend) >= 1:
                                do_transposes(pend.pop(0))
                            pend.append(job)
                        for job in pend:
                            do_transposes(job)
                        b.need(v, jobs[-1]["ex"], state.get("rcols_rd"))
                        for job in jobs:
                            b.inc(v.e.reciprocal(
                                out=cols[:, 16 + job["qsub"]:17 + job["qsub"]],
                                in_=cols[:, 32 + job["lix"]:33 + job["lix"]]), "vm")
                        do_pv(head, p0 + hs, qg, pts[0], 16)
        while filler:
            filler.pop(0)()

    def block_l1a(seq, final):
        b.dma_barrier(q, g, light=True)
        tls = ep_state.get("tiles", {})
        fallback = b.tok("vm")
        xftok = make_xfm(lambda ti: tls.get(ti, fallback))
        st = {"qkv_tok": None, "brel_tok": None}
        pre_rd = b.tok("tm")
        dtk = None
        for h in range(16):
            src = bass.AP(scr["erev"].tensor, h * 400 + 129, [[1, 128], [1, 128]])
            b.dma(g, bigB[:, 12288 + h * 128:12288 + (h + 1) * 128], src,
                  "gbl", waits=(pre_rd,) if h == 0 else ())
            srcd = bass.AP(scr["erev"].tensor, h * 400 + 1, [[1, 128], [1, 128]])
            dtk = b.dma(g, pbu[:, h * 128:(h + 1) * 128], srcd, "gbd",
                        waits=(pre_rd,) if h == 0 else ())
        btk = None
        for h in range(16):
            b.need(v, dtk, consts_tok)
            rap = bass.AP(pbu, h * 128 + 127, [[2048, 128], [-1, 128]])
            btk = b.inc(v.e.tensor_tensor(out=bigB[:, 14336 + h * 128:14336 + (h + 1) * 128],
                                          in0=smask[:, :], in1=rap, op=ALU.add), "vm")
        st["brel_tok"] = btk
        st["brel_low"] = b.tok("gbl")
        st["pbu_rd"] = btk

        def diag_bias(h):
            return bigB[:, 14336 + h * 128:14336 + (h + 1) * 128]

        def low_bias(h):
            return bass.AP(bigB, 12288 + h * 128 + 127, [[16384, 128], [-1, 128]])

        def far_col(h):
            return e0c[:, h:h + 1]

        def out_fm(h, qg):
            hp = h // 2
            return bigA[:, hp * 2048 + qg * 512:hp * 2048 + (qg + 1) * 512]

        def load_qkv(g2):
            base = (g2 % 2) * 3072
            toks = []
            for j, col0 in enumerate((g2 * 128, 1024 + g2 * 128, 2048 + g2 * 128)):
                dst = bass.AP(wbuf, base + j * 1024, [[16384, 128], [128, 8], [1, 128]])
                srcw = bass.AP(W["L1_attn_Wqkv"].tensor, col0,
                               [[3072, 128], [3072 * 128, 8], [1, 128]])
                toks.append(b.dma(g, dst, srcw, "gw",
                                  waits=(b.tok("tm"),) if j == 0 else ()))
            rbs = 1280 + (g2 % 2) * 384
            btk = None
            for j, col0 in enumerate((g2 * 128, 1024 + g2 * 128, 2048 + g2 * 128)):
                btk = b.dma(q, smal[0:1, rbs + j * 128:rbs + (j + 1) * 128],
                            W["L1_attn_bqkv"][0:1, col0:col0 + 128], "qr",
                            waits=(b.tok("tm"), b.tok("qr")) if j == 0 else ())
            return toks[-1], btk, rbs

        def qkv_fillers(g2, wtok, btk, rbs):
            base = (g2 % 2) * 3072
            bufb = (g2 % 2) * 6144
            fill = []
            for j, scale in ((0, True), (1, False)):
                for tg in range(4):
                    def go(j=j, tg=tg, scale=scale):
                        bank = pb[6]
                        b.need(t, wtok, btk, xftok, bank.rd)
                        for c in range(NCH):
                            t.e.matmul(bank.t[:, :],
                                       wbuf[:, base + j * 1024 + c * 128:
                                            base + j * 1024 + (c + 1) * 128],
                                       xsl(c, tg), start=(c == 0), stop=False)
                        ins = t.e.matmul(bank.t[:, :],
                                         smal[0:1, rbs + j * 128:rbs + (j + 1) * 128],
                                         sones[0:1, 0:512], start=False, stop=True)
                        mtk = b.inc(ins, "tm")
                        bank.wr = mtk
                        b.need(v, mtk)
                        dstc = bigB[:, bufb + j * 2048 + tg * 512:
                                    bufb + j * 2048 + (tg + 1) * 512]
                        if scale:
                            etk = b.inc(v.e.tensor_scalar(out=dstc, in0=bank.t[:, :],
                                                          scalar1=0.125, scalar2=None,
                                                          op0=ALU.mult), "vm")
                        else:
                            etk = b.inc(v.e.tensor_copy(out=dstc, in_=bank.t[:, :]), "vm")
                        bank.rd = etk
                    fill.append(go)
            for ti in range(NT):
                def gov(ti=ti):
                    bank = pb[6]
                    b.need(t, wtok, btk, xftok, bank.rd)
                    for c in range(NCH):
                        t.e.matmul(bank.t[:, 0:128], xslt(c, ti),
                                   wbuf[:, base + 2048 + c * 128:base + 2048 + (c + 1) * 128],
                                   start=(c == 0), stop=False)
                    ins = t.e.matmul(bank.t[:, 0:128], sones[0:1, 0:128],
                                     smal[0:1, rbs + 256:rbs + 384],
                                     start=False, stop=True)
                    mtk = b.inc(ins, "tm")
                    bank.wr = mtk
                    b.need(v, mtk)
                    etk = b.inc(v.e.tensor_copy(
                        out=bigB[:, bufb + 4096 + ti * 128:bufb + 4096 + (ti + 1) * 128],
                        in_=bank.t[:, 0:128]), "vm")
                    bank.rd = etk
                fill.append(gov)
            return fill

        w0, b0, rb0 = load_qkv(0)
        f0 = qkv_fillers(0, w0, b0, rb0)
        for go in f0:
            go()
        for g2 in range(8):
            st["qkv_tok"] = b.tok("vm")
            if g2 < 7:
                wn, bn, rbn = load_qkv(g2 + 1)
                fillers = qkv_fillers(g2 + 1, wn, bn, rbn)
            else:
                fillers = []
            bufb = (g2 % 2) * 6144
            heads = [(g2 * 2 + j, 64 * j, 0, j) for j in range(2)]
            attn_softmax_pv(seq, heads, T, bigB, bufb, bufb + 2048, bigB,
                            bufb + 4096, 128,
                            [(bigC, 0, 8192), (bigC, 4096, 8192)],
                            out_fm, diag_bias, low_bias, far_col, st,
                            (wbuf, 6144), wbuf[:, 8192:12288].bitcast(F32),
                            wbuf[:, 14336:16384].bitcast(F32),
                            wbuf[:, 12288:14336].bitcast(F32), filler=fillers)
        attn_tok = b.tok("vm")
        wo_rd = b.tok("tm")
        dst = bass.AP(wbuf, 0, [[16384, 128], [1024, 8], [1, 1024]])
        src = bass.AP(W["L1_attn_Wout"].tensor, 0, [[1024, 128], [1024 * 128, 8], [1, 1024]])
        wotk = b.dma(g, dst, src, "go", waits=(wo_rd, b.tok("vm"), b.tok("sm")))
        botk = b.dma(q, ssb[0:1, 1024:2048], W["L1_attn_bout"][0:1, :], "qr",
                     waits=(st.get("srd0"), st.get("srd1"), st.get("srd2"),
                            b.tok("qr")))
        for ti in range(NT):
            for ch in range(2):
                bank = pb[4 + ch]
                b.need(t, wotk, botk, attn_tok, bank.rd)
                for d in range(NCH):
                    t.e.matmul(bank.t[:, :],
                               bigA[:, d * 2048 + ti * 128:d * 2048 + (ti + 1) * 128],
                               wbuf[:, d * 1024 + ch * 512:d * 1024 + (ch + 1) * 512],
                               start=(d == 0), stop=False)
                ins = t.e.matmul(bank.t[:, :], sones[0:1, 0:128],
                                 ssb[0:1, 1024 + ch * 512:1024 + (ch + 1) * 512],
                                 start=False, stop=True)
                bank.wr = b.inc(ins, "tm")
            ln_epilogue(seq, ti, (pb[4].t[:, :], pb[4].wr), (pb[5].t[:, :], pb[5].wr),
                        final, banks=(pb[4], pb[5]))

    def block_l1f(seq, final):
        b.dma_barrier(q, g, light=True)
        tls = ep_state.get("tiles", {})
        fallback = b.tok("vm")
        xftok = make_xfm(lambda ti: tls.get(ti, fallback))
        sc_tok = None
        for i, sn in enumerate(("L1_ffn_w1_s", "L1_ffn_w2_s",
                                "L1_ffn_h1_s", "L1_ffn_h2_s")):
            stk = b.dma(q, ssb[0:32, 1024:1152],
                        bass.AP(W[sn].tensor, 0, [[128, 32], [1, 128]]),
                        "qsc", waits=(sc_tok or b.tok("tm"), b.tok("qsc")))
            bankx = pb[4 + (i % 2)]
            b.need(t, stk, bankx.rd, consts_tok)
            ttk = b.inc(t.e.transpose(bankx.t[:, 0:32], ssb[0:32, 1024:1152],
                                      fident[0:32, 0:32]), "tm")
            bankx.wr = ttk
            b.need(s, ttk)
            sc_tok = b.inc(s.e.copy(out=smal[:, 1024 + i * 32:1024 + (i + 1) * 32],
                                    in_=bankx.t[:, 0:32]), "sm")
            bankx.rd = sc_tok
        scl_tok = sc_tok
        slot_rd = [None] * 4
        cur = [0]

        def wload(src, width):
            i = cur[0] % 4
            cur[0] += 1
            sl = wbuf[:, i * 4096:i * 4096 + width]
            tk = b.dma(q, sl, src, "qs" + str(i), waits=(slot_rd[i],))
            return i, sl, tk

        for tg in range(4):
            for half, snm in ((0, "f1"), (1, "f2")):
                for f4 in range(8):
                    i, sl, wtk = wload(
                        bass.AP(scr[snm].tensor, f4 * 512 * 1024,
                                [[1024, 128], [128 * 1024, 4], [1, 1024]]), 4096)
                    for fj in range(4):
                        f = f4 * 4 + fj
                        bank = pb[f % 2]
                        b.need(t, wtk, xftok, bank.rd)
                        for c in range(NCH):
                            ins = t.e.matmul(bank.t[:, :], sl[:, fj * 1024 + c * 128:fj * 1024 + (c + 1) * 128],
                                             xsl(c, tg), start=(c == 0), stop=(c == NCH - 1))
                        mtk = b.inc(ins, "tm")
                        slot_rd[i] = mtk
                        bank.wr = mtk
                        dstc = bigA[:, f * 512:(f + 1) * 512]
                        scol = smal[:, 1024 + half * 32 + f:1025 + half * 32 + f]
                        if half == 0:
                            b.need(s, mtk, scl_tok)
                            etk = b.inc(s.e.activation(out=dstc, in_=bank.t[:, :],
                                                       func=AF.Silu, scale=scol), "sm")
                        else:
                            b.need(v, mtk, b.tok("sm"))
                            etk = b.inc(v.e.scalar_tensor_tensor(
                                out=dstc, in0=bank.t[:, :], scalar=scol, in1=dstc,
                                op0=ALU.mult, op1=ALU.mult), "vm")
                        bank.rd = etk
            for hn, srcb, dstb in (("h1", bigA, bigB), ("h2", bigB, bigA)):
                src_ready = (b.tok("vm"), b.tok("sm"))
                for f in range(32):
                    i, sl, wtk = wload(scr[hn][ts(f), :], 4096)
                    bank = pb[f % 2]
                    b.need(t, wtk, bank.rd, *(src_ready if f == 0 else ()))
                    for c in range(32):
                        ins = t.e.matmul(bank.t[:, :], sl[:, ts(c)],
                                         srcb[:, c * 512:(c + 1) * 512],
                                         start=(c == 0), stop=(c == 31))
                    mtk = b.inc(ins, "tm")
                    slot_rd[i] = mtk
                    bank.wr = mtk
                    b.need(s, mtk)
                    hcol = smal[:, 1024 + (64 if hn == "h1" else 96) + f:
                                1025 + (64 if hn == "h1" else 96) + f]
                    etk = b.inc(s.e.activation(out=dstb[:, f * 512:(f + 1) * 512],
                                               in_=bank.t[:, :], func=AF.Gelu,
                                               scale=hcol), "sm")
                    bank.rd = etk
            a2_ready = b.tok("sm")
            hold = [None] * 4
            for chalf in range(2):
                htok = None
                w3_waits = [slot_rd[0], slot_rd[1], slot_rd[2], slot_rd[3],
                            b.tok("tm") if chalf == 1 else None]
                gtoks = [None] * 4
                for gr in range(4):
                    for fj in range(8):
                        f = gr * 8 + fj
                        sl = wbuf[:, f * 512:(f + 1) * 512]
                        gtoks[gr] = b.dma(q, sl,
                                          scr["f3"][ts(f), chalf * 512:(chalf + 1) * 512],
                                          "qw3" + str(gr),
                                          waits=tuple(w3_waits) if f == 0 else ())
                for tt in range(4):
                    ti = tg * 4 + tt
                    bank = pb[tt] if chalf == 0 else pb[4 + (tt % 2)]
                    b.need(t, a2_ready, bank.rd)
                    for f in range(32):
                        b.need(t, gtoks[f // 8])
                        ins = t.e.matmul(bank.t[:, :],
                                         bigA[:, f * 512 + tt * 128: f * 512 + (tt + 1) * 128],
                                         wbuf[:, f * 512:(f + 1) * 512],
                                         start=(f == 0), stop=(f == 31))
                    bank.wr = b.inc(ins, "tm")
                    if chalf == 0:
                        hold[tt] = bank
                    else:
                        ln_epilogue(seq, ti, (hold[tt].t[:, 0:512], hold[tt].wr),
                                    (bank.t[:, 0:512], bank.wr), final,
                                    banks=(hold[tt], bank))
                wtm = b.tok("tm")
                for i in range(4):
                    slot_rd[i] = wtm
        return

    prepass()
    pp_toks = (b.tok("dgP"), b.tok("dgB"), b.tok("dgN0"), b.tok("dgN1"),
               b.tok("dgC"), b.tok("dgC3"), b.tok("dgA0"), b.tok("dgA1"), b.tok("vm"))
    b.need(q, *pp_toks)
    b.need(g, *pp_toks)
    for seq in range(n_seq):
        ltok = load_x(seq)
        for bi, blk in enumerate(blocks):
            final = (bi == len(blocks) - 1)
            if bi == 0:
                b.need(t, ltok)
                b.need(v, ltok)
            if blk == "L1F":
                block_l1f(seq, final)
            elif blk == "L0F":
                block_l0f(seq, final)
            elif blk == "L1A":
                block_l1a(seq, final)
            elif blk == "L0A":
                block_l0a(seq, final)
            else:
                raise NotImplementedError(blk)
    b.need(q, b.tok("dy"))
    b.need(g, b.tok("dy"))
    stack.close()
    return nc, in_names



def _consts():
    ident = np.eye(128, dtype=np.float32)
    mask = np.where(np.arange(128)[None, :] > np.arange(128)[:, None],
                    np.float32(NEG), np.float32(0.0)).astype(np.float32)
    return {"c_ident": ident, "c_mask": mask}


def _prep_inputs(inputs):
    out = {}
    for k, vv in inputs.items():
        a = np.asarray(vv)
        if a.ndim == 1:
            a = a[None, :]
        elif a.ndim == 3:
            a = a.reshape(a.shape[0] * a.shape[1], a.shape[2])
        out[k] = np.ascontiguousarray(a, dtype=np.float32)
    return out


def run_blocks(inputs, blocks=ALL_BLOCKS, x_override=None):
    xs = np.asarray(x_override if x_override is not None else inputs["L0_attn_x"],
                    dtype=np.float32)
    winp = _prep_inputs({k: v for k, v in inputs.items() if k != "L0_attn_x"})
    consts = _consts()
    nc, in_names = build_program(blocks)
    in_maps = []
    for c in range(NCORES):
        m = {"x": np.ascontiguousarray(xs[c * SEQN:(c + 1) * SEQN].reshape(SEQN * T, D))}
        for n in in_names:
            if n == "x":
                continue
            m[n] = consts[n] if n in consts else winp[n]
        in_maps.append(m)
    res = run_bass_kernel_spmd(nc, in_maps, core_ids=list(range(NCORES)))
    return np.concatenate([r["y"].reshape(SEQN, T, D) for r in res.results], axis=0)


def kernel(**inputs):
    return run_blocks(inputs, ALL_BLOCKS)
